# version 55
# baseline (speedup 1.0000x reference)
import numpy as np
import ml_dtypes
import concourse.bass as bass
import concourse.mybir as mybir
from concourse.bass_utils import run_bass_kernel_spmd

F32 = mybir.dt.float32
BF16 = mybir.dt.bfloat16
I8 = mybir.dt.int8
I32 = mybir.dt.int32
AF = mybir.ActivationFunctionType
ALU = mybir.AluOpType
AX = mybir.AxisListType

D = 1024
DL = 512
HD = 64
NPAIR = 4
TC = 512
EPS_LN = 1e-5
EPS_Q = 1e-6
INV127 = 1.0 / 127.0


class Sched:
    ENGS = ("sync", "gpsimd", "vector", "scalar", "tensor")

    def __init__(self):
        self.ops = {e: [] for e in self.ENGS}
        self.cnt = {s: 0 for s in (
            "ldx0", "ldx1", "ldx2", "ldx3", "ldx4", "ldx5", "ldw", "wcw", "wc8",
            "xr0", "xr1", "out0", "out1", "pe", "dv", "ac", "po")}

    OWN = {"vector": "dv", "scalar": "ac", "gpsimd": "po"}

    def emit(self, eng, fn, waits=(), inc=None, amt=1):
        if inc is None and eng in self.OWN:
            inc = self.OWN[eng]
        rec = None
        if inc is not None:
            self.cnt[inc] += amt
            rec = self.cnt[inc]
        ws = tuple((s, v) for (s, v) in waits if v)
        self.ops[eng].append((ws, fn, (inc, amt) if inc else None))
        return rec

    def replay(self, eng, handle, sems):
        seen = {}
        own = self.OWN.get(eng)
        prior = 0
        for waits, fn, inc in self.ops[eng]:
            allw = list(waits)
            if own is not None and prior > 0:
                allw.append((own, prior))
            for s, v in allw:
                if seen.get(s, 0) >= v:
                    continue
                handle.wait_ge(sems[s], v)
                seen[s] = v
            ins = fn(handle)
            if inc is not None:
                s, amt = inc
                ins.then_inc(sems[s], amt)
                if s == own:
                    prior += amt


def build_program(S=4096):
    NT = S // 128
    NTC = S // TC
    NJB = S // 128

    nc = bass.Bass("TRN2", target_bir_lowering=False)

    x_h = nc.dram_tensor("x", [S, D], F32, kind="ExternalInput").ap()
    xr_h = nc.dram_tensor("xr", [S, DL], F32, kind="ExternalInput").ap()
    wq_h = nc.dram_tensor("wq", [D, DL], F32, kind="ExternalInput").ap()
    wk_h = nc.dram_tensor("wk", [D, DL], F32, kind="ExternalInput").ap()
    wv_h = nc.dram_tensor("wv", [D, DL], F32, kind="ExternalInput").ap()
    w8_h = nc.dram_tensor("w8", [DL, D], I8, kind="ExternalInput").ap()
    wsc_h = nc.dram_tensor("wsc", [DL], F32, kind="ExternalInput").ap()
    bia_h = nc.dram_tensor("bia", [DL], F32, kind="ExternalInput").ap()
    idb_h = nc.dram_tensor("idb", [128, 128], BF16, kind="ExternalInput").ap()
    idf_h = nc.dram_tensor("idf", [128, 128], F32, kind="ExternalInput").ap()
    psw_h = nc.dram_tensor("psw", [128, 128], BF16, kind="ExternalInput").ap()
    nmk_h = nc.dram_tensor("nmk", [128, 128], BF16, kind="ExternalInput").ap()
    out_h = nc.dram_tensor("out", [S, DL], F32, kind="ExternalOutput").ap()

    from contextlib import ExitStack
    stack = ExitStack()
    sb = lambda name, shape, dt: stack.enter_context(nc.sbuf_tensor(name, shape, dt))
    ps = lambda name, shape, dt: stack.enter_context(nc.psum_tensor(name, shape, dt))

    qT = [sb(f"qT{p}", [128, 3 * TC], BF16) for p in range(NPAIR)]
    kTa = sb("kTa", [128, NPAIR, S], BF16)
    v_aug = sb("v_aug", [128, NJB, 520], BF16)
    gT = sb("gT", [128, 4, 4 * TC], BF16)
    t_xnB = sb("t_xnB", [128, 1024], F32)
    a2xq_d = sb("a2xq_d", [128, 3, 1024], BF16)
    t_xnC = sb("t_xnC", [128, 1024], F32)
    Wqk = sb("Wqk", [128, 8, 1024], BF16)
    Wv = sb("Wv", [128, 8, 512], BF16)
    wqT = sb("wqT", [128, 8, 512], BF16)
    w8n = sb("w8n", [128, 1024], BF16)
    wscb = sb("wscb", [128, 4], F32)
    biab = sb("biab", [128, 4], F32)
    idb = sb("idb_s", [128, 128], BF16)
    idf = sb("idf_s", [128, 128], F32)
    psw = sb("psw_s", [128, 128], BF16)
    nmk = sb("nmk_s", [128, 128], BF16)
    nhalf = sb("nhalf", [128, 1], F32)
    xst = sb("xst", [128, 3 * 512], F32)
    t_xn = sb("t_xn", [128, 1024], F32)
    t_b = sb("t_b", [128, 512], F32)
    t_g = sb("t_g", [128, 512], I32)
    t_r = sb("t_r", [128, 512], F32)
    xnbf = sb("xnbf", [128, 512], BF16)
    xqbf = sb("xqbf", [128, 512], BF16)
    xnb2 = [xnbf, sb("xnbf1", [128, 512], BF16)]
    xqb2a = [xqbf, sb("xqbf1", [128, 512], BF16)]
    xnq = sb("xnq", [128, 16, TC], BF16)
    g1s = sb("g1s", [128, 512], F32)
    xscBs = sb("xscBs", [128, 512], F32)
    rows = sb("rows", [128, 1024], F32)
    bnst = sb("bnst", [128, 12], F32)
    stat = sb("stat", [128, 8], F32)
    negx = sb("negx", [128, NT], F32)
    statq = sb("statq", [128, NT, 1], F32)
    statc = sb("statc", [128, NT, 2], F32)
    xnK = [sb(f"xnK{i}", [128, 8, 128], BF16) for i in range(2)]
    ktmp2 = [sb(f"ktmp{i}", [128, 512], BF16) for i in range(2)]
    bnst2 = [sb(f"bnst{i}", [128, 12], F32) for i in range(2)]
    stat2 = [sb(f"stat{i}", [128, 8], F32) for i in range(2)]
    eT = [sb(f"eT{i}", [128, 1024], BF16) for i in range(3)]
    wpr = sb("wpr", [128, 512], BF16)
    linvT = sb("linvT", [128, 8], F32)
    outs = [sb(f"outs{i}", [128, 512], F32) for i in range(2)]
    xres = [sb(f"xres{i}", [128, 512], F32) for i in range(2)]

    sT = [ps(f"sT{i}", [128, 1024], F32) for i in range(2)]
    Oa = ps("Oa", [65, 512], F32)
    Ob = ps("Ob", [65, 512], F32)
    pm = ps("pm", [128, 512], F32)
    pt = ps("pt", [128, 1024], BF16)

    sems = {n: stack.enter_context(nc.semaphore(n))
            for n in ("ldx0", "ldx1", "ldx2", "ldx3", "ldx4", "ldx5",
                      "ldw", "wcw", "wc8",
                      "xr0", "xr1", "out0", "out1", "pe", "dv", "ac", "po")}
    sc = Sched()

    dep = {
        "w_qkv": None, "w_qT": None, "w_small": None,
        "qkT": [None] * NTC, "v": [None] * NTC, "g": [None] * NTC,
        "xnT_po": [0] * NTC, "xnT_ac": [0] * NTC,
        "xnT_ready": [None] * NTC, "xqT_ready": [None] * NTC,
        "pm_free": ("dv", 0), "pt_free": ("dv", 0), "pt_user": 0,
        "v_ones": 0, "negxsc_row": 0, "xscB_ready": ("dv", 0),
        "rows0_rd": 0, "rows64_rd": 0, "g1_free": 0, "wpr_free": 0,
        "stat_free": [0, 0], "k_all": 0, "xnb_rd": [0, 0], "xnK_rd": [0, 0],
        "slot_free": [0, 0, 0, 0], "slot_free_b": [0, 0], "negx_last": 0,
        "a1_scatter": None, "ktmp_rd": [0, 0], "a2buf_rd": {}, "a2q": {}, "t_xn_free": [0, 0, 0],
        "xnK_rd2": [0, 0], "xnq_rd": 0, "xnK_wr": [0, 0],
        "gT_rd": [0, 0, 0, 0], "negx_rd": 0, "t_b_rd": 0, "z_po": 0, "a2_x32": ("po", 0),
        "c_st": {}, "a2aff": {}, "a2q_h": {}, "a2qpo_h": {}, "inv_po": 0,
        "att_q_free": [[0, 0, 0] for _ in range(NPAIR)],
        "xres_free": [0, 0], "outs_free": [0, 0],
    }
    astate = {"xdma": [None] * (4 * NT), "half_done": [None] * (4 * NT),
              "slot_last": {}}
    att = {"sT_rd": [0, 0], "eT_rd": [0, 0, 0], "O_rd": 0, "unit": 0,
           "pend_gate": []}

    def emit_weights():
        wq_r = wq_h.rearrange("(kk p) f -> p kk f", p=128)
        wk_r = wk_h.rearrange("(kk p) f -> p kk f", p=128)
        wv_r = wv_h.rearrange("(kk p) f -> p kk f", p=128)
        sc.emit("gpsimd", lambda g: g.dma_start(out=Wqk[:, :, 0:512], in_=wq_r),
                inc="wcw", amt=16)
        sc.emit("gpsimd", lambda g: g.dma_start(out=Wqk[:, :, 512:1024], in_=wk_r),
                inc="wcw", amt=16)
        dep["w_qkv"] = sc.emit("gpsimd", lambda g: g.dma_start(out=Wv[:], in_=wv_r),
                               inc="wcw", amt=16)
        for oc in range(4):
            sc.emit("sync", lambda s, oc=oc: s.dma_start(
                out=wscb[:, oc:oc + 1],
                in_=wsc_h[128 * oc:128 * oc + 128].rearrange("(p a) -> p a", a=1)),
                inc="ldw", amt=16)
            sc.emit("sync", lambda s, oc=oc: s.dma_start(
                out=biab[:, oc:oc + 1],
                in_=bia_h[128 * oc:128 * oc + 128].rearrange("(p a) -> p a", a=1)),
                inc="ldw", amt=16)
        sc.emit("sync", lambda s: s.dma_start(out=idb[:], in_=idb_h), inc="ldw", amt=16)
        sc.emit("sync", lambda s: s.dma_start(out=idf[:], in_=idf_h), inc="ldw", amt=16)
        sc.emit("sync", lambda s: s.dma_start(out=psw[:], in_=psw_h), inc="ldw", amt=16)
        dep["w_small"] = sc.emit("sync", lambda s: s.dma_start(out=nmk[:], in_=nmk_h),
                                 inc="ldw", amt=16)
        sc.emit("gpsimd", lambda g: g.memset(nhalf[:], -0.5), inc="po")
        dep["ones_r"] = sc.emit("vector", lambda v: v.memset(
            rows[0:1, 512:640], 1.0), inc="dv")
        dep["v_ones"] = sc.emit("vector", lambda v: v.memset(
            v_aug[:, :, 64:520:65], 2.0), inc="dv")
        last_ev = None
        for qt in range(4):
            w8q = w8_h[128 * qt:128 * qt + 128, :]
            c_ld = sc.emit("gpsimd", lambda g, w8q=w8q: g.dma_start(out=w8n[:], in_=w8q),
                           waits=(("dv", last_ev),) if last_ev else (), inc="wc8", amt=16)
            c_tr = None
            for kk in range(8):
                c_tr = sc.emit("tensor", lambda t, kk=kk: t.transpose(
                    pt[:, 128 * kk:128 * kk + 128], w8n[:, 128 * kk:128 * kk + 128], idb[:]),
                    waits=(("wc8", c_ld), ("ldw", dep["w_small"]),
                           dep["pt_free"]) if kk == 0 else (),
                    inc="pe" if kk == 7 else None)
            dep["pt_user"] = c_tr
            last_ev = sc.emit("vector", lambda v, qt=qt: v.tensor_copy(
                wqT[:, :, 128 * qt:128 * qt + 128],
                pt[:].rearrange("p (kk f) -> p kk f", kk=8)),
                waits=(("pe", c_tr),), inc="dv")
            dep["pt_free"] = ("dv", last_ev)
        dep["w_qT"] = last_ev

    def emit_x_dma(h):
        i = (h % (2 * NT)) // 2
        buf = h % 3
        prev = astate["slot_last"].get(buf)
        semn = f"ldx{h % 3}"
        cnt = sc.emit("sync", lambda s, i=i, h=h, buf=buf: s.dma_start(
            out=xst[:, 512 * buf:512 * buf + 512],
            in_=x_h[128 * i:128 * i + 128, 512 * (h % 2):512 * (h % 2) + 512]),
            waits=(prev,) if prev else (), inc=semn, amt=16)
        astate["xdma"][h] = (semn, cnt)

    def emit_a1_stats(i):
        par = i % 2
        bnst = bnst2[par]
        stat = stat2[par]
        for h2 in (0, 1):
            buf = (2 * i + h2) % 3
            sc.emit("vector", lambda v, buf=buf, h2=h2, bnst=bnst: v.bn_stats(
                bnst[:, 6 * h2:6 * h2 + 6], xst[:, 512 * buf:512 * buf + 512]),
                waits=(astate["xdma"][2 * i + h2],
                       ("po", dep["stat_free"][par])), inc="dv")
        sc.emit("vector", lambda v, bnst=bnst, stat=stat: v.bn_aggr(stat[:, 0:2], bnst[:]),
                inc="dv")
        dep["c_st"][i] = sc.cnt["dv"]

    def emit_tile_a1(i):
        par = i % 2
        bnst = bnst2[par]
        stat = stat2[par]
        xK = xnK[par]
        ktmp = ktmp2[par]
        slotA = sT[par][:, 0:512]
        slotB = sT[par][:, 512:1024]
        c_st = dep["c_st"][i]
        sc.emit("gpsimd", lambda g, stat=stat: g.tensor_scalar_add(
            stat[:, 7:8], stat[:, 1:2], EPS_LN), waits=(("dv", c_st),))
        c_sq = sc.emit("gpsimd", lambda g, stat=stat, i=i: g.tensor_tensor(
            statc[:, i, 0:1], stat[:, 7:8], nhalf[:], ALU.pow), inc="po")
        dep["stat_free"][par] = c_sq
        c_nmr = ("po", sc.emit("gpsimd", lambda g, i=i, stat=stat: g.tensor_scalar(
            statc[:, i, 1:2], stat[:, 0:1], statc[:, i, 0:1], -1.0,
            ALU.mult, ALU.mult), inc="po"))
        xb = [xnb2[par], xqb2a[par]]
        for h2 in (0, 1):
            buf = (2 * i + h2) % 3
            astate["half_done"][2 * i + h2] = ("po", sc.emit(
                "gpsimd", lambda g, buf=buf, h2=h2, i=i, xb=xb: g.tensor_scalar(
                    xb[h2][:], xst[:, 512 * buf:512 * buf + 512],
                    statc[:, i, 0:1], statc[:, i, 1:2], ALU.mult, ALU.add),
                waits=(c_nmr, ("pe", dep["xnb_rd"][par])), inc="po"))
            astate["slot_last"][buf] = astate["half_done"][2 * i + h2]
        c_xb = ("po", astate["half_done"][2 * i + 1][1])
        if dep["a1_scatter"] is not None:
            dep["a1_scatter"]()
            dep["a1_scatter"] = None
        c_tr = None
        for h2 in (0, 1):
            for j in range(4):
                c_tr = sc.emit("tensor", lambda t, h2=h2, j=j, xb=xb: t.transpose(
                    pt[:, 512 * h2 + 128 * j:512 * h2 + 128 * j + 128],
                    xb[h2][:, 128 * j:128 * j + 128], idb[:]),
                    waits=(c_xb, dep["pt_free"],
                           ("ldw", dep["w_small"])) if (h2 == 0 and j == 0) else (),
                    inc="pe" if (h2 == 1 and j == 3) else None)
        dep["pt_user"] = c_tr
        dep["xnb_rd"][par] = c_tr
        c_ev = sc.emit("scalar", lambda s, xK=xK: s.activation(
            xK[:].rearrange("p k f -> p (k f)"), pt[:], AF.Copy),
            waits=(("pe", c_tr), ("pe", dep["xnK_rd"][par]),
                   ("po", dep["xnK_rd2"][par])), inc="ac")
        dep["xnK_wr"][par] = c_ev
        dep["pt_free"] = ("ac", c_ev)
        G = S // 64
        m_lo, m_hi = 2 * i, 2 * i + 1
        c_mm = None
        for kk in range(8):
            w = (("ac", c_ev), ("wcw", dep["w_qkv"]),
                 ("dv", dep["slot_free"][2 * par])) if kk == 0 else ()
            c_mm = sc.emit("tensor", lambda t, kk=kk, xK=xK, slotA=slotA: t.matmul(
                slotA, xK[:, kk, :], Wqk[:, kk, 512:1024],
                start=(kk == 0), stop=(kk == 7)), waits=w, inc="pe" if kk == 7 else None)
        dep["xnK_rd"][par] = c_mm
        c_kt = sc.emit("scalar", lambda s, ktmp=ktmp, slotA=slotA: s.activation(
            ktmp[:], slotA, AF.Copy), waits=(("pe", c_mm),), inc="ac")
        c_sw = sc.emit("tensor", lambda t, ktmp=ktmp, slotB=slotB: t.matmul(
            slotB, psw[:], ktmp[:], start=True, stop=True),
            waits=(("dv", dep["slot_free_b"][par]), ("ac", c_kt)),
            inc="pe")
        dep["ktmp_rd"][par] = c_sw

        def scatter(par=par, m_lo=m_lo, c_sw=c_sw, c_kt=c_kt):
            full = sT[par][:]
            kv = kTa[:, :, :].rearrange("p q (d m) -> p q d m", m=G)
            src = full.rearrange("p (sl h e) -> p h e sl", sl=2, h=4)
            srcL = full.rearrange("p (sl he) -> p sl he", sl=2) \
                       .rearrange("p sl (h e) -> p sl h e", h=8)[0:64, :, 0::2, :] \
                       .transpose([0, 2, 3, 1])
            srcU = full.rearrange("p (sl he) -> p sl he", sl=2) \
                       .rearrange("p sl (h e) -> p sl h e", h=8)[64:128, :, 1::2, :] \
                       .transpose([0, 2, 3, 1])[:, :, :, ::-1]
            c_e2 = sc.emit("vector", lambda v: v.tensor_copy(
                kv[0:64, :, :, m_lo:m_lo + 2], srcL),
                waits=(("pe", c_sw), ("ac", c_kt)), inc="dv")
            dep["slot_free"][2 * par] = c_e2
            c_e4 = sc.emit("vector", lambda v: v.tensor_copy(
                kv[64:128, :, :, m_lo:m_lo + 2], srcU), inc="dv")
            dep["slot_free_b"][par] = c_e4
        dep["a1_scatter"] = scatter

    NMERGE = min(4, NTC)

    def a2k(i):
        return ("d", i % 3) if i < 4 * NMERGE else ("o", i % 4)

    def a2_xn(i, h2):
        par = i % 4
        if par < 2:
            return xnK[par][:].rearrange("p k f -> p (k f)")[:, 512 * h2:512 * h2 + 512]
        return Wqk[:, 2 * (par - 2) + h2, 512:1024]

    def a2_xq(i, h2):
        if i < 4 * NMERGE:
            return a2xq_d[:, i % 3, 512 * h2:512 * h2 + 512]
        par = i % 4
        return [ktmp2[0][:], ktmp2[1][:], w8n[:, 0:512], w8n[:, 512:1024],
                Wqk[:, 4, 512:1024], Wqk[:, 5, 512:1024],
                Wqk[:, 6, 512:1024], Wqk[:, 7, 512:1024]][2 * par + h2]

    dep["t_g_rd"] = 0

    def emit_a2_affine(i, merged):
        txn = [t_xn, t_xnB, t_xnC][i % 3]
        c_x32 = None
        for h2 in (0, 1):
            h = (2 * i + h2) if merged else (2 * NT + 2 * i + h2)
            buf = h % 3
            w = [("dv", dep["t_xn_free"][i % 3])]
            if not merged:
                w.append(astate["xdma"][h])
            c_x32 = sc.emit(
                "gpsimd", lambda g, buf=buf, h2=h2, i=i, txn=txn: g.tensor_scalar(
                    txn[:, 512 * h2:512 * h2 + 512], xst[:, 512 * buf:512 * buf + 512],
                    statc[:, i, 0:1], statc[:, i, 1:2], ALU.mult, ALU.add),
                waits=tuple(w), inc="po")
            astate["slot_last"][buf] = ("po", c_x32)
        dep["a2_x32"] = ("po", c_x32)
        dep["a2aff"][i] = c_x32

    def emit_tile_a2_stats(i, merged=False):
        stat = stat2[i % 2]
        txn = [t_xn, t_xnB, t_xnC][i % 3]
        if not merged:
            emit_a2_affine(i, False)
            return
        c_x32 = dep["a2aff"][i]
        sc.emit("vector", lambda v, stat=stat, txn=txn: v.reduce_max(
            stat[:, 4:5], txn[:], AX.X, apply_absolute_value=True),
            waits=(("po", c_x32),), inc="dv")
        sc.emit("vector", lambda v, stat=stat: v.tensor_scalar_max(
            stat[:, 4:5], stat[:, 4:5], EPS_Q), inc="dv")
        sc.emit("vector", lambda v, stat=stat: v.reciprocal(stat[:, 5:6], stat[:, 4:5]),
                inc="dv")
        sc.emit("vector", lambda v, stat=stat: v.tensor_scalar_mul(
            stat[:, 5:6], stat[:, 5:6], 127.0), inc="dv")
        sc.emit("vector", lambda v, stat=stat: v.tensor_scalar_mul(
            stat[:, 6:7], stat[:, 4:5], -INV127), inc="dv")
        dep["negx_last"] = ("dv", sc.emit("vector", lambda v, i=i, stat=stat:
            v.tensor_copy(negx[:, i:i + 1], stat[:, 6:7]),
            waits=(("pe", dep["negx_rd"]),), inc="dv"))

    def emit_prestats(i):
        stat = stat2[i % 2]
        txn = [t_xn, t_xnB, t_xnC][i % 3]
        c_x32 = dep["a2aff"][i]
        sc.emit("vector", lambda v, stat=stat, txn=txn: v.reduce_max(
            stat[:, 4:5], txn[:], AX.X, apply_absolute_value=True),
            waits=(("po", c_x32),), inc="dv")
        sc.emit("vector", lambda v, stat=stat: v.tensor_scalar_max(
            stat[:, 4:5], stat[:, 4:5], EPS_Q), inc="dv")
        sc.emit("vector", lambda v, stat=stat: v.reciprocal(stat[:, 5:6], stat[:, 4:5]),
                inc="dv")
        sc.emit("vector", lambda v, i=i, stat=stat: v.tensor_scalar_mul(
            statq[:, i, 0:1], stat[:, 5:6], 127.0), inc="dv")
        sc.emit("vector", lambda v, i=i, stat=stat: v.tensor_scalar_mul(
            negx[:, i:i + 1], stat[:, 4:5], -INV127), inc="dv")
        dep["t_xn_free"][i % 3] = sc.cnt["dv"]
        dep["negx_last"] = ("dv", sc.cnt["dv"])

    def emit_tile_a2_quant(i, halves=(0, 1), parts=(0, 1)):
        stat = stat2[i % 2]
        txn = [t_xn, t_xnB, t_xnC][i % 3]
        dep.setdefault("a2qpo", {})
        for h2 in halves:
            off = 512 * h2
            if 0 in parts:
                if i >= 4 * NMERGE:
                    w = [("pe", dep["a2buf_rd"].get(a2k(i), 0)),
                         ("pe", dep["xnK_rd"][i % 2]), ("pe", dep["ktmp_rd"][i % 2])]
                    sc.emit("gpsimd", lambda g, off=off, i=i, h2=h2, txn=txn:
                            g.tensor_copy(a2_xn(i, h2), txn[:, off:off + 512]),
                            waits=tuple(w), inc="po")
                inv = stat[:, 5:6] if i < 4 * NMERGE else statq[:, i, 0:1]
                if i < 4 * NMERGE:
                    wg = (("pe", dep["a2buf_rd"].get(a2k(i), 0)),)
                else:
                    wg = (dep["a2_x32"],) if h2 == 0 else ()
                sc.emit("vector", lambda v, off=off, inv=inv, txn=txn:
                        v.tensor_scalar_mul(
                    t_g[:], txn[:, off:off + 512], inv), waits=wg, inc="dv")
                sc.emit("vector", lambda v, off=off, inv=inv, txn=txn:
                        v.scalar_tensor_tensor(
                    t_r[:], txn[:, off:off + 512], inv, t_g[:],
                    ALU.mult, ALU.subtract), inc="dv")
            if 1 in parts:
                sc.emit("vector", lambda v: v.scalar_tensor_tensor(
                    t_b[:], t_r[:], 0.5, t_g[:], ALU.is_ge, ALU.add),
                    waits=(("ac", dep["g1_free"]), ("po", dep["z_po"])), inc="dv")
                sc.emit("vector", lambda v, i=i, h2=h2: v.scalar_tensor_tensor(
                    a2_xq(i, h2), t_r[:], -0.5, t_b[:], ALU.is_le, ALU.subtract),
                    inc="dv")
                dep["a2q_h"].setdefault(i, {})[h2] = sc.cnt["dv"]
                dep["a2qpo_h"].setdefault(i, {})[h2] = sc.cnt["po"]
        dep["a2qpo"][i] = sc.cnt["po"]
        dep["t_xn_free"][i % 3] = sc.cnt["dv"]
        dep["t_b_rd"] = sc.cnt["dv"]
        return sc.cnt["dv"]

    def emit_tile_a2_pe(i):
        c_q = dep["a2q"][i]
        c_qpo = dep["a2qpo"][i]
        tcol = (i % 4) * 128
        c_tr = None
        if i < 4 * NMERGE:
            for h2 in (0, 1):
                for j in range(4):
                    c_tr = sc.emit("tensor", lambda t, j=j, h2=h2, i=i: t.transpose(
                        pt[:, 512 * h2 + 128 * j:512 * h2 + 128 * j + 128],
                        a2_xq(i, h2)[:, 128 * j:128 * j + 128], idb[:]),
                        waits=(("dv", c_q), ("po", c_qpo), dep["pt_free"])
                        if (h2 == 0 and j == 0) else (),
                        inc="pe" if (h2 == 1 and j == 3) else None)
            dep["pt_user"] = c_tr
            c_ev = None
            for k in range(8):
                c_ev = sc.emit("scalar", lambda s, k=k, tcol=tcol: s.activation(
                    xnq[:, 8 + k, tcol:tcol + 128],
                    pt[:, 128 * k:128 * k + 128], AF.Copy),
                    waits=(("pe", c_tr),) if k == 0 else (),
                    inc="ac" if k == 7 else None)
            dep["pt_free"] = ("ac", c_ev)
        else:
            for h2 in (0, 1):
                emit_tile_a2_pe_h(i, h2)
            return
        dep["a2buf_rd"][a2k(i)] = dep["pt_user"]

    def emit_tile_a2_pe_h(i, h2):
        c_q = dep["a2q_h"][i][h2]
        c_qpo = dep["a2qpo_h"][i][h2]
        tcol = (i % 4) * 128
        c_tr = None
        for j in range(4):
            c_tr = sc.emit("tensor", lambda t, j=j, h2=h2, i=i: t.transpose(
                pt[:, 128 * j:128 * j + 128],
                a2_xn(i, h2)[:, 128 * j:128 * j + 128],
                idb[:]),
                waits=(("dv", c_q), ("po", c_qpo), dep["pt_free"]) if j == 0 else ())
            c_tr = sc.emit("tensor", lambda t, j=j, h2=h2, i=i: t.transpose(
                pt[:, 512 + 128 * j:512 + 128 * j + 128],
                a2_xq(i, h2)[:, 128 * j:128 * j + 128], idb[:]),
                inc="pe" if j == 3 else None)
        dep["pt_user"] = c_tr
        c_ev = sc.emit("vector", lambda v, h2=h2, tcol=tcol: v.tensor_copy(
            xnq[:, :, tcol:tcol + 128]
                .rearrange("p (g k) f -> p g k f", g=2)[:, :, 4 * h2:4 * h2 + 4, :],
            pt[:].rearrange("p (g k f) -> p g k f", g=2, k=4)),
            waits=(("pe", c_tr),), inc="dv")
        dep["pt_free"] = ("dv", c_ev)
        if h2 == 1:
            dep["a2buf_rd"][a2k(i)] = dep["pt_user"]

    def emit_xn_copy(i):
        par = i % 2
        tcol = (i % 4) * 128
        c_cp = sc.emit("gpsimd", lambda g, par=par, tcol=tcol: g.tensor_copy(
            xnq[:, 0:8, tcol:tcol + 128], xnK[par][:]),
            waits=(("ac", dep["xnK_wr"][par]), ("pe", dep["xnq_rd"])), inc="po")
        dep["xnK_rd2"][par] = c_cp

    def emit_negxsc_row(tc):
        c_mm = None
        for j in range(4):
            sl = 4 * tc + j
            c_mm = sc.emit("tensor", lambda t, j=j, sl=sl: t.matmul(
                pm[0:1, 128 * j:128 * j + 128], negx[:, sl:sl + 1], idf[:],
                start=True, stop=True),
                waits=(dep["negx_last"], dep["pm_free"],
                       ("pe", dep["rows0_rd"]), ("ldw", dep["w_small"])) if j == 0 else (),
                inc="pe" if j == 3 else None)
        c_rv = sc.emit("vector", lambda v: v.tensor_copy(rows[0:1, 0:512], pm[0:1, 0:512]),
                       waits=(("pe", c_mm),), inc="dv")
        dep["pm_free"] = ("dv", c_rv)
        dep["negxsc_row"] = c_rv
        dep["negx_rd"] = c_mm

    def emit_qkv_f(tc, f, half):
        xr = dep["xnT_ready"][tc]
        c_mm = None
        for kk in range(4 * half, 4 * half + 4):
            w = ()
            if kk == 0:
                w = [("dv", xr), ("po", dep["xnT_po"][tc]),
                     ("ac", dep["xnT_ac"][tc]),
                     ("wcw", dep["w_qkv"]), dep["pm_free"],
                     ("pe", dep["att_q_free"][f][tc % 3])]
            c_mm = sc.emit("tensor", lambda t, kk=kk, f=f: t.matmul(
                pm[:], Wqk[:, kk, 128 * f:128 * f + 128], xnq[:, kk, :],
                start=(kk == 0), stop=(kk == 7)), waits=w, inc="pe" if kk == 7 else None)
        if half == 0:
            return
        dep["xnq_rd"] = c_mm
        c_ev = sc.emit("scalar", lambda s, f=f, tc=tc: s.activation(
            qT[f][:, (tc % 3) * TC:(tc % 3) * TC + TC], pm[:], AF.Copy, scale=0.125),
            waits=(("pe", c_mm),), inc="ac")
        dep["pm_free"] = ("ac", c_ev)

    def emit_v_sub(tc, vs, half):
        xr = dep["xnT_ready"][tc]
        c_mm = None
        for kk in range(4 * half, 4 * half + 4):
            w = (("dv", xr), ("po", dep["xnT_po"][tc]), ("ac", dep["xnT_ac"][tc]),
                 ("wcw", dep["w_qkv"]), dep["pm_free"]) if kk == 0 else ()
            c_mm = sc.emit("tensor", lambda t, kk=kk, vs=vs: t.matmul(
                pm[:], xnq[:, kk, 128 * vs:128 * vs + 128], Wv[:, kk, :],
                start=(kk == 0), stop=(kk == 7)), waits=w, inc="pe" if kk == 7 else None)
        if half == 0:
            return
        dep["xnq_rd"] = c_mm
        jb = 4 * tc + vs
        c_ev = sc.emit("vector", lambda v, jb=jb: v.tensor_copy(
            v_aug[:, jb, :].rearrange("p (h e) -> p h e", h=8)[:, :, 0:64],
            pm[:].rearrange("p (h e) -> p h e", h=8)),
            waits=(("pe", c_mm), ("dv", dep["v_ones"])), inc="dv")
        dep["pm_free"] = ("dv", c_ev)

    def emit_ffn_pre(tc):
        c_mm = sc.emit("tensor", lambda t: t.matmul(
            pm[:], rows[0:1, 512:640], rows[0:1, 0:512], start=True, stop=True),
            waits=(("dv", max(dep["negxsc_row"], dep["ones_r"])), dep["pm_free"],
                   ("ldw", dep["w_small"])), inc="pe")
        dep["rows0_rd"] = c_mm
        c_cp = sc.emit("scalar", lambda s: s.activation(xscBs[:], pm[:], AF.Copy),
                       waits=(("pe", c_mm),), inc="ac")
        dep["pm_free"] = ("ac", c_cp)
        dep["xscB_ready"] = ("ac", c_cp)

    def emit_ffn_oc(tc, oc, half):
        xr = dep["xqT_ready"][tc]
        c_mm = None
        for kk in range(4 * half, 4 * half + 4):
            w = (("dv", xr), ("ac", dep["xnT_ac"][tc]), ("dv", dep["w_qT"]),
                 dep["pm_free"], dep["xscB_ready"]) if kk == 0 else ()
            c_mm = sc.emit("tensor", lambda t, kk=kk, oc=oc: t.matmul(
                pm[:], wqT[:, kk, 128 * oc:128 * oc + 128], xnq[:, 8 + kk, :],
                start=(kk == 0), stop=(kk == 7)), waits=w, inc="pe" if kk == 7 else None)
        if half == 0:
            return
        c_g1 = sc.emit("vector", lambda v: v.tensor_tensor(
            g1s[:], pm[:], xscBs[:], ALU.mult),
            waits=(("pe", c_mm), ("ac", dep["g1_free"])), inc="dv")
        dep["pm_free"] = ("dv", c_g1)
        c_z = sc.emit("gpsimd", lambda g, oc=oc: g.tensor_scalar(
            t_b[:], g1s[:], wscb[:, oc:oc + 1], biab[:, oc:oc + 1],
            ALU.mult, ALU.add),
            waits=(("dv", max(c_g1, dep["t_b_rd"])), ("ac", dep["g1_free"])),
            inc="po")
        dep["z_po"] = c_z
        c_th = sc.emit("scalar", lambda s: s.activation(
            g1s[:], t_b[:], AF.Tanh, scale=0.5),
            waits=(("po", c_z), ("ldw", dep["w_small"])), inc="ac")
        dep["g1_free"] = c_th
        gsl = TC * (tc % 4)
        c_si = sc.emit("vector", lambda v, oc=oc, gsl=gsl: v.scalar_tensor_tensor(
            gT[:, oc, gsl:gsl + TC], g1s[:], 1.0, t_b[:],
            ALU.add, ALU.mult),
            waits=(("ac", c_th), ("dv", dep["gT_rd"][tc % 4])), inc="dv")
        dep["t_b_rd"] = c_si
        return c_si

    def qq(i):
        dep["a2q"][i] = emit_tile_a2_quant(i)

    def qq2(i):
        dep["a2q"][i] = emit_tile_a2_quant(i, (1,))

    def a_rest_steps(tc):
        steps = []
        for j in range(4):
            if tc < NMERGE:
                steps.append((lambda i=4 * tc + j: emit_xn_copy(i), None,
                              4 * tc + j + 1))
                steps.append((lambda i=4 * tc + j: emit_tile_a2_pe(i), None,
                              4 * tc + j + 2))
            else:
                steps.append((lambda i=4 * tc + j: emit_tile_a2_pe_h(i, 0), None, 0))
                steps.append((lambda i=4 * tc + j: emit_tile_a2_pe_h(i, 1), None, 0))

        def fin_tiles(tc=tc):
            dep["xnT_ready"][tc] = sc.cnt["dv"]
            dep["xqT_ready"][tc] = sc.cnt["dv"]
            dep["xnT_po"][tc] = sc.cnt["po"]
            dep["xnT_ac"][tc] = sc.cnt["ac"]
            emit_negxsc_row(tc)
        steps.append((fin_tiles, f"tilesdone{tc}", 4 * tc + 5))
        for vs in range(4):
            g = 4 * tc + 5
            steps.append((lambda vs=vs, tc=tc: emit_v_sub(tc, vs, 0), "open", g))
            steps.append((lambda vs=vs, tc=tc: emit_v_sub(tc, vs, 1), None, g))

        def fin_v(tc=tc):
            dep["v"][tc] = sc.cnt["dv"]
        steps.append((fin_v, f"vdone{tc}", 0))
        steps.append((lambda tc=tc: emit_ffn_pre(tc), None, 0))
        for oc in range(4):
            g = 4 * tc + 5
            steps.append((lambda oc=oc, tc=tc: emit_ffn_oc(tc, oc, 0), "open", g))
            steps.append((lambda oc=oc, tc=tc: emit_ffn_oc(tc, oc, 1), None, g))

        def fin_g(tc=tc):
            dep["g"][tc] = sc.cnt["dv"]
        steps.append((fin_g, None, 0))
        return steps

    def a_q_steps(tc):
        steps = []
        for f in range(4):
            steps.append((lambda f=f, tc=tc: emit_qkv_f(tc, f, 0), "open",
                          4 * tc + 5))
            steps.append((lambda f=f, tc=tc: emit_qkv_f(tc, f, 1), None,
                          4 * tc + 5))

        def fin_qk(tc=tc):
            dep["qkT"][tc] = sc.cnt["ac"]
        steps.append((fin_qk, f"qkdone{tc}", 0))
        return steps

    def a_substeps(tc):
        steps = []
        tiles = [4 * tc + j for j in range(4)]

        def dma_pair(i):
            emit_x_dma(2 * NT + 2 * i)
            emit_x_dma(2 * NT + 2 * i + 1)

        def sd(j):
            i = tiles[j]
            dma_pair(i)
            emit_tile_a2_stats(i)
        rest = [(fn, tag) if tag else fn for fn, tag, _g in a_rest_steps(tc)]
        pes, rest = rest[:8], rest[8:]
        order = [("s", 0), ("qa", 0), ("qb", 0), ("s", 1), ("qa", 1),
                 ("p", 0), ("qb", 1), ("p", 1), ("s", 2), ("qa", 2),
                 ("p", 2), ("qb", 2), ("s", 3), ("p", 3), ("qa", 3),
                 ("p", 4), ("qb", 3), ("p", 5), ("p", 6), ("p", 7)]
        for kind, j in order:
            if kind == "s":
                steps.append(lambda j=j: sd(j))
            elif kind == "qa":
                steps.append(lambda i=tiles[j]: emit_tile_a2_quant(i, (0,), (0,)))
                steps.append(lambda i=tiles[j]: emit_tile_a2_quant(i, (0,), (1,)))
            elif kind == "qb":
                steps.append(lambda i=tiles[j]: emit_tile_a2_quant(i, (1,), (0,)))
                steps.append(lambda i=tiles[j]: emit_tile_a2_quant(i, (1,), (1,)))
            else:
                steps.append(pes[j])
        for st in rest:
            steps.append(st)
        stepsB = [(fn, tag) if tag else fn for fn, tag, _g in a_q_steps(tc)]
        return steps, stepsB

    def emit_attention_chunk(c, pump, close_open, drain_pre, drain_full):
        for P in range(NPAIR):
            if P == 1:
                drain_full()
            njb = 4 * c + 4
            c_sc_last = None
            pend_av = []
            last_av = [None]

            def flush_av():
                if pend_av:
                    pend_av.pop(0)()

            pops_at = (0, 1, 2, 3) if njb <= 4 else (0, 2, 3, 4)
            for jb in range(njb):
                if P == 0 and jb == 4 * c - 1:
                    drain_pre()
                if att["pend_gate"] and jb in pops_at:
                    close_open()
                    att["pend_gate"].pop(0)()
                if jb >= 3:
                    flush_av()
                par = att["unit"] % 2
                pe3 = att["unit"] % 3
                o = max(0, 128 * jb - TC * c)
                w = TC - o
                is_diag = jb >= 4 * c
                qb = (c % 3) * TC
                w0 = (("ac", max(att["sT_rd"][par], dep["qkT"][c], dep["k_all_ac"])),
                      ("dv", dep["k_all"]))
                if not is_diag:
                    sc.emit("tensor", lambda t, P=P, jb=jb, o=o, w=w, par=par, qb=qb:
                            t.matmul(
                        sT[par][:, o:o + w],
                        kTa[0:64, P, 128 * jb:128 * jb + 128],
                        qT[P][0:64, qb + o:qb + o + w],
                        start=True, stop=True), waits=w0)
                    c_sc = sc.emit("tensor", lambda t, P=P, jb=jb, o=o, w=w, par=par,
                                   qb=qb: t.matmul(
                        sT[par][:, 512 + o:512 + o + w],
                        kTa[64:128, P, 128 * jb:128 * jb + 128],
                        qT[P][64:128, qb + o:qb + o + w],
                        start=True, stop=True), inc="pe")
                else:
                    sc.emit("tensor", lambda t, P=P, jb=jb, o=o, par=par, qb=qb:
                            t.matmul(
                        sT[par][:, o:o + 128],
                        kTa[0:64, P, 128 * jb:128 * jb + 128],
                        qT[P][0:64, qb + o:qb + o + 128],
                        start=True, stop=False), waits=w0)
                    sc.emit("tensor", lambda t, o=o, par=par: t.matmul(
                        sT[par][:, o:o + 128], idb[:], nmk[:],
                        start=False, stop=True))
                    sc.emit("tensor", lambda t, P=P, jb=jb, o=o, par=par, qb=qb:
                            t.matmul(
                        sT[par][:, 512 + o:512 + o + 128],
                        kTa[64:128, P, 128 * jb:128 * jb + 128],
                        qT[P][64:128, qb + o:qb + o + 128],
                        start=True, stop=False))
                    c_sc = sc.emit("tensor", lambda t, o=o, par=par: t.matmul(
                        sT[par][:, 512 + o:512 + o + 128], idb[:], nmk[:],
                        start=False, stop=True), inc=None if w > 128 else "pe")
                    if w > 128:
                        sc.emit("tensor", lambda t, P=P, jb=jb, o=o, w=w, par=par,
                                qb=qb: t.matmul(
                            sT[par][:, o + 128:o + w],
                            kTa[0:64, P, 128 * jb:128 * jb + 128],
                            qT[P][0:64, qb + o + 128:qb + o + w],
                            start=True, stop=True))
                        c_sc = sc.emit("tensor", lambda t, P=P, jb=jb, o=o, w=w,
                                       par=par, qb=qb: t.matmul(
                            sT[par][:, 512 + o + 128:512 + o + w],
                            kTa[64:128, P, 128 * jb:128 * jb + 128],
                            qT[P][64:128, qb + o + 128:qb + o + w],
                            start=True, stop=True), inc="pe")
                c_sc_last = c_sc
                c_ex = sc.emit("scalar", lambda s, par=par, pe3=pe3, o=o, w=w:
                               s.activation(
                    eT[pe3][:].rearrange("p (b n) -> p b n", b=2)[:, :, o:o + w],
                    sT[par][:].rearrange("p (b n) -> p b n", b=2)[:, :, o:o + w],
                    AF.Exp),
                    waits=(("pe", c_sc), ("pe", att["eT_rd"][pe3])), inc="ac")
                att["sT_rd"][par] = c_ex
                avw = [("ac", c_ex), ("dv", dep["v"][jb // 4])]
                hA, hB = 2 * P, 2 * P + 1

                def mk_av(jb=jb, o=o, w=w, pe3=pe3, hA=hA, hB=hB, njb=njb, avw=tuple(avw)):
                    def go():
                        ws = list(avw)
                        if jb == 0:
                            ws.append(("dv", att["O_rd"]))
                        sc.emit("tensor", lambda t: t.matmul(
                            Oa[:, o:o + w], v_aug[:, jb, 65 * hA:65 * hA + 65],
                            eT[pe3][:, o:o + w],
                            start=(jb == 0), stop=(jb == njb - 1)), waits=ws)
                        c_av = sc.emit("tensor", lambda t: t.matmul(
                            Ob[:, o:o + w], v_aug[:, jb, 65 * hB:65 * hB + 65],
                            eT[pe3][:, 512 + o:512 + o + w],
                            start=(jb == 0), stop=(jb == njb - 1)), inc="pe")
                        att["eT_rd"][pe3] = c_av
                        last_av[0] = c_av
                    return go
                pend_av.append(mk_av())
                att["unit"] += 1
                pump()
            while pend_av:
                pend_av.pop(0)()
            dep["att_q_free"][P][c % 3] = c_sc_last
            gst = {}

            def ga1(c=c, P=P, av=last_av[0], gst=gst):
                gst["lw"] = emit_gating_a1(c, P, av)

            def ga2(c=c, P=P, gst=gst):
                gst["le"] = emit_gating_a2(c, P, gst["lw"][0])

            def gb1(c=c, P=P, gst=gst):
                gst["xr"] = emit_gating_b1(c, P)

            def gb2(c=c, P=P, gst=gst):
                emit_gating_b2(c, P, gst["le"], gst["xr"], gst["lw"][1])
            att["pend_gate"] += [ga1, ga2, gb1, gb2]

    def emit_gating_a1(c, P, c_av_last):
        sc.emit("scalar", lambda s: s.activation(
            rows[64:65, 0:512], Oa[64:65, :], AF.Copy),
            waits=(("pe", c_av_last), ("pe", dep["rows64_rd"])))
        c_l = sc.emit("scalar", lambda s: s.activation(
            rows[64:65, 512:1024], Ob[64:65, :], AF.Copy), inc="ac")
        gsl = TC * (c % 4)
        sc.emit("vector", lambda v, P=P, gsl=gsl: v.tensor_tensor(
            wpr[0:64, :], Oa[0:64, :], gT[0:64, P, gsl:gsl + TC], ALU.mult),
            waits=(("pe", c_av_last), ("dv", dep["g"][c]),
                   ("pe", dep["wpr_free"])))
        c_w = sc.emit("vector", lambda v, P=P, gsl=gsl: v.tensor_tensor(
            wpr[64:128, :], Ob[0:64, :], gT[64:128, P, gsl:gsl + TC], ALU.mult),
            inc="dv")
        att["O_rd"] = c_w
        if P == NPAIR - 1:
            dep["gT_rd"][c % 4] = c_w
        return c_l, c_w

    def emit_gating_a2(c, P, c_l):
        c_lt = None
        for it in range(4):
            c_lt = sc.emit("tensor", lambda t, it=it: t.matmul(
                pm[0:128, 2 * it:2 * it + 1], rows[64:65, 128 * it:128 * it + 128],
                idf[64:65, 64:65], start=True, stop=True),
                waits=(("ac", c_l), dep["pm_free"]) if it == 0 else ())
            c_lt = sc.emit("tensor", lambda t, it=it: t.matmul(
                pm[0:128, 2 * it + 1:2 * it + 2],
                rows[64:65, 512 + 128 * it:512 + 128 * it + 128],
                idf[64:65, 64:65], start=True, stop=True),
                inc="pe" if it == 3 else None)
        dep["rows64_rd"] = c_lt
        c_le = sc.emit("vector", lambda v: v.reciprocal(linvT[:, 0:8], pm[0:128, 0:8]),
                       waits=(("pe", c_lt),), inc="dv")
        dep["pm_free"] = ("dv", c_le)
        return c_le

    def emit_gating_b1(c, P):
        par = (c * NPAIR + P) % 2
        c_xr = sc.emit("gpsimd", lambda g, c=c, P=P, par=par: g.dma_start(
            out=xres[par][:].rearrange("p (it f) -> p it f", it=4),
            in_=xr_h[TC * c:TC * c + TC, :].rearrange("(it p) d -> p it d", p=128)
                 [:, :, 128 * P:128 * P + 128]),
            waits=(("dv", dep["xres_free"][par]),), inc=f"xr{par}", amt=16)
        return c_xr

    def emit_gating_b2(c, P, c_le, c_xr, c_w):
        par = (c * NPAIR + P) % 2
        c_wt = None
        for it in range(4):
            c_wt = sc.emit("tensor", lambda t, it=it: t.transpose(
                pt[:, 128 * it:128 * it + 128], wpr[:, 128 * it:128 * it + 128], idb[:]),
                waits=(("dv", c_w), dep["pt_free"]) if it == 0 else (),
                inc="pe" if it == 3 else None)
        dep["pt_user"] = c_wt
        dep["wpr_free"] = c_wt
        c_f = None
        for it in range(4):
            c_f = sc.emit("vector", lambda v, it=it, par=par: v.scalar_tensor_tensor(
                outs[par][:, 128 * it:128 * it + 64],
                pt[:, 128 * it:128 * it + 64], linvT[:, 2 * it:2 * it + 1],
                xres[par][:, 128 * it:128 * it + 64], ALU.mult, ALU.add),
                waits=(("pe", c_wt), (f"xr{par}", c_xr), ("dv", c_le),
                       (f"out{par}", dep["outs_free"][par])) if it == 0 else ())
            c_f = sc.emit("vector", lambda v, it=it, par=par: v.scalar_tensor_tensor(
                outs[par][:, 128 * it + 64:128 * it + 128],
                pt[:, 128 * it + 64:128 * it + 128], linvT[:, 2 * it + 1:2 * it + 2],
                xres[par][:, 128 * it + 64:128 * it + 128], ALU.mult, ALU.add),
                inc="dv" if it == 3 else None)
        dep["pt_free"] = ("dv", c_f)
        dep["xres_free"][par] = c_f
        c_out = sc.emit("gpsimd", lambda g, c=c, P=P, par=par: g.dma_start(
            out=out_h[TC * c:TC * c + TC, :].rearrange("(it p) d -> p it d", p=128)
                 [:, :, 128 * P:128 * P + 128],
            in_=outs[par][:].rearrange("p (it f) -> p it f", it=4)),
            waits=(("dv", c_f),), inc=f"out{par}", amt=16)
        dep["outs_free"][par] = c_out

    emit_weights()
    M = []
    for tc in range(NMERGE):
        M += a_rest_steps(tc)
        if tc < 3:
            M += a_q_steps(tc)
    mst = {"k": 0}

    def pump_merged(i, drain=False):
        target = len(M) if drain else min(
            len(M), (max(0, i - 2) * len(M)) // max(1, NT + 6))
        while mst["k"] < len(M):
            fn, tag, gate = M[mst["k"]]
            tag = tag if tag == "open" else None
            if not drain and gate > i - 1 and not (mst["k"] < target and gate <= i):
                break
            fn()
            mst["k"] += 1
            while mst["k"] < len(M) and M[mst["k"] - 1][1] == "open":
                M[mst["k"]][0]()
                mst["k"] += 1

    emit_x_dma(0)
    emit_x_dma(1)
    emit_x_dma(2)
    emit_a1_stats(0)
    for i in range(NT):
        pump_merged(i)
        emit_tile_a1(i)
        emit_a2_affine(i, True)
        for h in (2 * i + 3, 2 * i + 4):
            if h < 2 * NT:
                emit_x_dma(h)
        if i + 1 < NT:
            emit_a1_stats(i + 1)
        if 1 <= i and i - 1 < 4 * NMERGE:
            qq(i - 1)
        if i < 4 * NMERGE:
            emit_tile_a2_stats(i, merged=True)
        else:
            emit_prestats(i)
    for k in range(NT - 1, 4 * NMERGE):
        qq(k)
    if dep["a1_scatter"] is not None:
        dep["a1_scatter"]()
        dep["a1_scatter"] = None
    dep["k_all"] = sc.cnt["dv"]
    dep["k_all_ac"] = 0

    def norm(raw):
        return [e if isinstance(e, tuple) else (e, None) for e in raw]

    QA = []
    qa_end = {}
    flatA = []
    qb = {}

    qa_tiles_end = {}
    qa_pre_end = {}

    def append_tc(tc):
        sa, sb = a_substeps(tc)
        sa = norm(sa)
        for j, (fn, tag) in enumerate(sa):
            if tag and tag.startswith("tilesdone"):
                qa_tiles_end[tc] = len(flatA) + j + 1
                sa[j] = (fn, None)
            elif tag and tag.startswith("vdone"):
                qa_pre_end[tc] = len(flatA) + j + 1
                sa[j] = (fn, None)
        flatA.extend(sa)
        qa_end[tc] = len(flatA)
        qb[tc] = norm(sb)

    for tcq in range(3, NMERGE):
        if tcq < NTC:
            qb[tcq] = norm([(fn, tag) if tag else fn
                            for fn, tag, _g in a_q_steps(tcq)])
    qa_start = {}
    if mst["k"] < len(M):
        if NTC <= 2:
            pump_merged(NT, drain=True)
        else:
            for fn, tag, _g in M[mst["k"]:]:
                keep = tag if tag == "open" else None
                flatA.append((fn, keep))
                if tag and tag.startswith("qkdone"):
                    t = int(tag[6:])
                    qa_start[t] = qa_pre_end[t] = qa_end[t] = len(flatA)
                elif tag and tag.startswith("vdone"):
                    qa_pre_end[int(tag[5:])] = len(flatA)
                elif tag and tag.startswith("tilesdone"):
                    qa_tiles_end[int(tag[9:])] = len(flatA)
            mst["k"] = len(M)
            qa_end[min(NMERGE, NTC) - 1] = len(flatA)
    ia = {"k": 0}
    ib = {"k": 0, "lst": []}

    def stepA():
        fn, tag = flatA[ia["k"]]
        fn()
        ia["k"] += 1
        return tag

    def stepB():
        fn, tag = ib["lst"][ib["k"]]
        fn()
        ib["k"] += 1
        return tag

    def drain_open():
        while (ia["k"] > 0 and ia["k"] < len(flatA)
               and flatA[ia["k"] - 1][1] == "open"):
            stepA()
        while (ib["k"] > 0 and ib["k"] < len(ib["lst"])
               and ib["lst"][ib["k"] - 1][1] == "open"):
            stepB()

    for c in range(NTC):
        def drain_pre(c=c):
            while ia["k"] < qa_pre_end.get(c, qa_end.get(c, 0)):
                stepA()

        def drain_full(c=c):
            while ia["k"] < qa_end.get(c, 0):
                stepA()
        while ia["k"] < qa_start.get(c, 0):
            stepA()
        if c == 0:
            drain_pre()
        while ib["k"] < len(ib["lst"]):
            stepB()
        if NMERGE <= c + 2 < NTC:
            append_tc(c + 2)
        ib["lst"] = qb.get(c + 2, [])
        ib["k"] = 0
        ib["gate"] = qa_tiles_end.get(c + 2, 0)

        uc = {"u": 0}
        nunits = NPAIR * (4 * c + 4)
        dlA = 0

        def pump():
            uc["u"] += 1
            left = max(1, nunits - uc["u"] + 1)
            needed = 0
            budget = 1
            for _ in range(budget):
                if ib["k"] < len(ib["lst"]) and ia["k"] >= ib["gate"]:
                    stepB()
                elif ia["k"] < len(flatA):
                    stepA()
                else:
                    break
            drain_open()

        def close_open():
            drain_open()
        emit_attention_chunk(c, pump, close_open, drain_pre, drain_full)
    while ia["k"] < len(flatA):
        stepA()
    while ib["k"] < len(ib["lst"]):
        stepB()

    while att["pend_gate"]:
        att["pend_gate"].pop(0)()
    sc.emit("gpsimd", lambda g: g.engine_nop(),
            waits=(("out0", sc.cnt["out0"]), ("out1", sc.cnt["out1"])))

    with nc.Block() as block:
        @block.sync
        def _(s):
            sc.replay("sync", s, sems)

        @block.gpsimd
        def _(g):
            sc.replay("gpsimd", g, sems)

        @block.vector
        def _(v):
            sc.replay("vector", v, sems)

        @block.scalar
        def _(s):
            sc.replay("scalar", s, sems)

        @block.tensor
        def _(t):
            sc.replay("tensor", t, sems)

    stack.close()
    return nc


_NC_CACHE = {}


def _get_nc(S):
    if S not in _NC_CACHE:
        _NC_CACHE[S] = build_program(S)
    return _NC_CACHE[S]


def kernel(L0_attn_x, L0_attn_qkv_weights, L0_ffn_weight_q, L0_ffn_weight_scale,
           L0_ffn_bias):
    x = np.asarray(L0_attn_x)
    W = np.asarray(L0_attn_qkv_weights)
    w8 = np.asarray(L0_ffn_weight_q)
    wsc = np.asarray(L0_ffn_weight_scale)
    bia = np.asarray(L0_ffn_bias)
    B, S, D_ = x.shape
    assert D_ == D

    ident_b = np.eye(128, dtype=ml_dtypes.bfloat16)
    ident_f = np.eye(128, dtype=np.float32)
    j = np.arange(128)
    maskT = (j[:, None] <= j[None, :]).astype(ml_dtypes.bfloat16)
    onesf = np.ones((128, 128), dtype=np.float32)
    pswap = np.zeros((128, 128), dtype=ml_dtypes.bfloat16)
    pswap[(np.arange(128) + 64) % 128, np.arange(128)] = 1
    negmask = np.where(j[:, None] > j[None, :], -1e9, 0.0).astype(ml_dtypes.bfloat16)

    nc = _get_nc(S)
    in_maps = []
    for c in range(8):
        b, half = c // 2, c % 2
        hs = DL * half
        in_maps.append({
            "x": np.ascontiguousarray(x[b]),
            "xr": np.ascontiguousarray(x[b][:, hs:hs + DL]),
            "wq": np.ascontiguousarray(W[:, hs:hs + DL]),
            "wk": np.ascontiguousarray(W[:, D + hs:D + hs + DL]),
            "wv": np.ascontiguousarray(W[:, 2 * D + hs:2 * D + hs + DL]),
            "w8": np.ascontiguousarray(w8[hs:hs + DL, :]),
            "wsc": np.ascontiguousarray(wsc[hs:hs + DL]),
            "bia": np.ascontiguousarray(bia[hs:hs + DL]),
            "idb": ident_b, "idf": ident_f,
            "psw": pswap, "nmk": negmask,
        })
    res = run_bass_kernel_spmd(nc, in_maps, core_ids=list(range(8)))
    out = np.empty((B, S, D), dtype=np.float32)
    for c in range(8):
        b, half = c // 2, c % 2
        out[b][:, DL * half:DL * half + DL] = np.asarray(res.results[c]["out"])
    return out


# version 56
# speedup vs baseline: 1.0023x; 1.0023x over previous
import numpy as np
import ml_dtypes
import concourse.bass as bass
import concourse.mybir as mybir
from concourse.bass_utils import run_bass_kernel_spmd

F32 = mybir.dt.float32
BF16 = mybir.dt.bfloat16
I8 = mybir.dt.int8
I32 = mybir.dt.int32
AF = mybir.ActivationFunctionType
ALU = mybir.AluOpType
AX = mybir.AxisListType

D = 1024
DL = 512
HD = 64
NPAIR = 4
TC = 512
EPS_LN = 1e-5
EPS_Q = 1e-6
INV127 = 1.0 / 127.0


class Sched:
    ENGS = ("sync", "gpsimd", "vector", "scalar", "tensor")

    def __init__(self):
        self.ops = {e: [] for e in self.ENGS}
        self.cnt = {s: 0 for s in (
            "ldx0", "ldx1", "ldx2", "ldx3", "ldx4", "ldx5", "ldw", "wcw", "wc8",
            "xr0", "xr1", "out0", "out1", "pe", "dv", "ac", "po")}

    OWN = {"vector": "dv", "scalar": "ac", "gpsimd": "po"}

    def emit(self, eng, fn, waits=(), inc=None, amt=1):
        if inc is None and eng in self.OWN:
            inc = self.OWN[eng]
        rec = None
        if inc is not None:
            self.cnt[inc] += amt
            rec = self.cnt[inc]
        ws = tuple((s, v) for (s, v) in waits if v)
        self.ops[eng].append((ws, fn, (inc, amt) if inc else None))
        return rec

    def replay(self, eng, handle, sems):
        seen = {}
        own = self.OWN.get(eng)
        prior = 0
        for waits, fn, inc in self.ops[eng]:
            allw = list(waits)
            if own is not None and prior > 0:
                allw.append((own, prior))
            for s, v in allw:
                if seen.get(s, 0) >= v:
                    continue
                handle.wait_ge(sems[s], v)
                seen[s] = v
            ins = fn(handle)
            if inc is not None:
                s, amt = inc
                ins.then_inc(sems[s], amt)
                if s == own:
                    prior += amt


def build_program(S=4096):
    NT = S // 128
    NTC = S // TC
    NJB = S // 128

    nc = bass.Bass("TRN2", target_bir_lowering=False)

    x_h = nc.dram_tensor("x", [S, D], F32, kind="ExternalInput").ap()
    xr_h = nc.dram_tensor("xr", [S, DL], F32, kind="ExternalInput").ap()
    wq_h = nc.dram_tensor("wq", [D, DL], F32, kind="ExternalInput").ap()
    wk_h = nc.dram_tensor("wk", [D, DL], F32, kind="ExternalInput").ap()
    wv_h = nc.dram_tensor("wv", [D, DL], F32, kind="ExternalInput").ap()
    w8_h = nc.dram_tensor("w8", [DL, D], I8, kind="ExternalInput").ap()
    wsc_h = nc.dram_tensor("wsc", [DL], F32, kind="ExternalInput").ap()
    bia_h = nc.dram_tensor("bia", [DL], F32, kind="ExternalInput").ap()
    idb_h = nc.dram_tensor("idb", [128, 128], BF16, kind="ExternalInput").ap()
    idf_h = nc.dram_tensor("idf", [128, 128], F32, kind="ExternalInput").ap()
    psw_h = nc.dram_tensor("psw", [128, 128], BF16, kind="ExternalInput").ap()
    nmk_h = nc.dram_tensor("nmk", [128, 128], BF16, kind="ExternalInput").ap()
    out_h = nc.dram_tensor("out", [S, DL], F32, kind="ExternalOutput").ap()

    from contextlib import ExitStack
    stack = ExitStack()
    sb = lambda name, shape, dt: stack.enter_context(nc.sbuf_tensor(name, shape, dt))
    ps = lambda name, shape, dt: stack.enter_context(nc.psum_tensor(name, shape, dt))

    qT = [sb(f"qT{p}", [128, 3 * TC], BF16) for p in range(NPAIR)]
    kTa = sb("kTa", [128, NPAIR, S], BF16)
    v_aug = sb("v_aug", [128, NJB, 520], BF16)
    gT = sb("gT", [128, 4, 4 * TC], BF16)
    t_xnB = sb("t_xnB", [128, 1024], F32)
    a2xq_d = sb("a2xq_d", [128, 3, 1024], BF16)
    t_xnC = sb("t_xnC", [128, 1024], F32)
    Wqk = sb("Wqk", [128, 8, 1024], BF16)
    Wv = sb("Wv", [128, 8, 512], BF16)
    wqT = sb("wqT", [128, 8, 512], BF16)
    w8n = sb("w8n", [128, 1024], BF16)
    wscb = sb("wscb", [128, 4], F32)
    biab = sb("biab", [128, 4], F32)
    idb = sb("idb_s", [128, 128], BF16)
    idf = sb("idf_s", [128, 128], F32)
    psw = sb("psw_s", [128, 128], BF16)
    nmk = sb("nmk_s", [128, 128], BF16)
    nhalf = sb("nhalf", [128, 1], F32)
    xst = sb("xst", [128, 3 * 512], F32)
    t_xn = sb("t_xn", [128, 1024], F32)
    t_b = sb("t_b", [128, 512], F32)
    t_g = sb("t_g", [128, 512], I32)
    t_r = sb("t_r", [128, 512], F32)
    xnbf = sb("xnbf", [128, 512], BF16)
    xqbf = sb("xqbf", [128, 512], BF16)
    xnb2 = [xnbf, sb("xnbf1", [128, 512], BF16)]
    xqb2a = [xqbf, sb("xqbf1", [128, 512], BF16)]
    xnq = sb("xnq", [128, 16, TC], BF16)
    g1s = sb("g1s", [128, 512], F32)
    xscBs = sb("xscBs", [128, 512], F32)
    rows = sb("rows", [128, 1024], F32)
    bnst = sb("bnst", [128, 12], F32)
    stat = sb("stat", [128, 8], F32)
    negx = sb("negx", [128, NT], F32)
    statq = sb("statq", [128, NT, 1], F32)
    statc = sb("statc", [128, NT, 2], F32)
    xnK = [sb(f"xnK{i}", [128, 8, 128], BF16) for i in range(2)]
    ktmp2 = [sb(f"ktmp{i}", [128, 512], BF16) for i in range(2)]
    bnst2 = [sb(f"bnst{i}", [128, 12], F32) for i in range(2)]
    stat2 = [sb(f"stat{i}", [128, 8], F32) for i in range(2)]
    eT = [sb(f"eT{i}", [128, 1024], BF16) for i in range(3)]
    wpr = sb("wpr", [128, 512], BF16)
    linvT = sb("linvT", [128, 8], F32)
    outs = [sb(f"outs{i}", [128, 512], F32) for i in range(2)]
    xres = [sb(f"xres{i}", [128, 512], F32) for i in range(2)]

    sT = [ps(f"sT{i}", [128, 1024], F32) for i in range(2)]
    Oa = ps("Oa", [65, 512], F32)
    Ob = ps("Ob", [65, 512], F32)
    pm = ps("pm", [128, 512], F32)
    pt = ps("pt", [128, 1024], BF16)

    sems = {n: stack.enter_context(nc.semaphore(n))
            for n in ("ldx0", "ldx1", "ldx2", "ldx3", "ldx4", "ldx5",
                      "ldw", "wcw", "wc8",
                      "xr0", "xr1", "out0", "out1", "pe", "dv", "ac", "po")}
    sc = Sched()

    dep = {
        "w_qkv": None, "w_qT": None, "w_small": None,
        "qkT": [None] * NTC, "v": [None] * NTC, "g": [None] * NTC,
        "xnT_po": [0] * NTC, "xnT_ac": [0] * NTC,
        "xnT_ready": [None] * NTC, "xqT_ready": [None] * NTC,
        "pm_free": ("dv", 0), "pt_free": ("dv", 0), "pt_user": 0,
        "v_ones": 0, "negxsc_row": 0, "xscB_ready": ("dv", 0),
        "rows0_rd": 0, "rows64_rd": 0, "g1_free": 0, "wpr_free": 0,
        "stat_free": [0, 0], "k_all": 0, "xnb_rd": [0, 0], "xnK_rd": [0, 0],
        "slot_free": [0, 0, 0, 0], "slot_free_b": [0, 0], "negx_last": 0,
        "a1_scatter": None, "ktmp_rd": [0, 0], "a2buf_rd": {}, "a2q": {}, "t_xn_free": [0, 0, 0],
        "xnK_rd2": [0, 0], "xnq_rd": 0, "xnK_wr": [0, 0],
        "gT_rd": [0, 0, 0, 0], "negx_rd": 0, "t_b_rd": 0, "z_po": 0, "a2_x32": ("po", 0),
        "c_st": {}, "a2aff": {}, "a2q_h": {}, "a2qpo_h": {}, "inv_po": 0,
        "att_q_free": [[0, 0, 0] for _ in range(NPAIR)],
        "xres_free": [0, 0], "outs_free": [0, 0],
    }
    astate = {"xdma": [None] * (4 * NT), "half_done": [None] * (4 * NT),
              "slot_last": {}}
    att = {"sT_rd": [0, 0], "eT_rd": [0, 0, 0], "O_rd": 0, "unit": 0,
           "pend_gate": []}

    def emit_weights():
        wq_r = wq_h.rearrange("(kk p) f -> p kk f", p=128)
        wk_r = wk_h.rearrange("(kk p) f -> p kk f", p=128)
        wv_r = wv_h.rearrange("(kk p) f -> p kk f", p=128)
        sc.emit("gpsimd", lambda g: g.dma_start(out=Wqk[:, :, 0:512], in_=wq_r),
                inc="wcw", amt=16)
        sc.emit("gpsimd", lambda g: g.dma_start(out=Wqk[:, :, 512:1024], in_=wk_r),
                inc="wcw", amt=16)
        dep["w_qkv"] = sc.emit("gpsimd", lambda g: g.dma_start(out=Wv[:], in_=wv_r),
                               inc="wcw", amt=16)
        for oc in range(4):
            sc.emit("sync", lambda s, oc=oc: s.dma_start(
                out=wscb[:, oc:oc + 1],
                in_=wsc_h[128 * oc:128 * oc + 128].rearrange("(p a) -> p a", a=1)),
                inc="ldw", amt=16)
            sc.emit("sync", lambda s, oc=oc: s.dma_start(
                out=biab[:, oc:oc + 1],
                in_=bia_h[128 * oc:128 * oc + 128].rearrange("(p a) -> p a", a=1)),
                inc="ldw", amt=16)
        sc.emit("sync", lambda s: s.dma_start(out=idb[:], in_=idb_h), inc="ldw", amt=16)
        sc.emit("sync", lambda s: s.dma_start(out=idf[:], in_=idf_h), inc="ldw", amt=16)
        sc.emit("sync", lambda s: s.dma_start(out=psw[:], in_=psw_h), inc="ldw", amt=16)
        dep["w_small"] = sc.emit("sync", lambda s: s.dma_start(out=nmk[:], in_=nmk_h),
                                 inc="ldw", amt=16)
        sc.emit("gpsimd", lambda g: g.memset(nhalf[:], -0.5), inc="po")
        dep["ones_r"] = sc.emit("vector", lambda v: v.memset(
            rows[0:1, 512:640], 1.0), inc="dv")
        dep["v_ones"] = sc.emit("vector", lambda v: v.memset(
            v_aug[:, :, 64:520:65], 2.0), inc="dv")
        last_ev = None
        for qt in range(4):
            w8q = w8_h[128 * qt:128 * qt + 128, :]
            c_ld = sc.emit("gpsimd", lambda g, w8q=w8q: g.dma_start(out=w8n[:], in_=w8q),
                           waits=(("dv", last_ev),) if last_ev else (), inc="wc8", amt=16)
            c_tr = None
            for kk in range(8):
                c_tr = sc.emit("tensor", lambda t, kk=kk: t.transpose(
                    pt[:, 128 * kk:128 * kk + 128], w8n[:, 128 * kk:128 * kk + 128], idb[:]),
                    waits=(("wc8", c_ld), ("ldw", dep["w_small"]),
                           dep["pt_free"]) if kk == 0 else (),
                    inc="pe" if kk == 7 else None)
            dep["pt_user"] = c_tr
            last_ev = sc.emit("vector", lambda v, qt=qt: v.tensor_copy(
                wqT[:, :, 128 * qt:128 * qt + 128],
                pt[:].rearrange("p (kk f) -> p kk f", kk=8)),
                waits=(("pe", c_tr),), inc="dv")
            dep["pt_free"] = ("dv", last_ev)
        dep["w_qT"] = last_ev

    def emit_x_dma(h):
        i = (h % (2 * NT)) // 2
        buf = h % 3
        prev = astate["slot_last"].get(buf)
        semn = f"ldx{h % 3}"
        cnt = sc.emit("sync", lambda s, i=i, h=h, buf=buf: s.dma_start(
            out=xst[:, 512 * buf:512 * buf + 512],
            in_=x_h[128 * i:128 * i + 128, 512 * (h % 2):512 * (h % 2) + 512]),
            waits=(prev,) if prev else (), inc=semn, amt=16)
        astate["xdma"][h] = (semn, cnt)

    def emit_a1_stats(i):
        par = i % 2
        bnst = bnst2[par]
        stat = stat2[par]
        for h2 in (0, 1):
            buf = (2 * i + h2) % 3
            sc.emit("vector", lambda v, buf=buf, h2=h2, bnst=bnst: v.bn_stats(
                bnst[:, 6 * h2:6 * h2 + 6], xst[:, 512 * buf:512 * buf + 512]),
                waits=(astate["xdma"][2 * i + h2],
                       ("po", dep["stat_free"][par])), inc="dv")
        sc.emit("vector", lambda v, bnst=bnst, stat=stat: v.bn_aggr(stat[:, 0:2], bnst[:]),
                inc="dv")
        dep["c_st"][i] = sc.cnt["dv"]

    def emit_tile_a1(i):
        par = i % 2
        bnst = bnst2[par]
        stat = stat2[par]
        xK = xnK[par]
        ktmp = ktmp2[par]
        slotA = sT[par][:, 0:512]
        slotB = sT[par][:, 512:1024]
        c_st = dep["c_st"][i]
        sc.emit("gpsimd", lambda g, stat=stat: g.tensor_scalar_add(
            stat[:, 7:8], stat[:, 1:2], EPS_LN), waits=(("dv", c_st),))
        c_sq = sc.emit("gpsimd", lambda g, stat=stat, i=i: g.tensor_tensor(
            statc[:, i, 0:1], stat[:, 7:8], nhalf[:], ALU.pow), inc="po")
        dep["stat_free"][par] = c_sq
        c_nmr = ("po", sc.emit("gpsimd", lambda g, i=i, stat=stat: g.tensor_scalar(
            statc[:, i, 1:2], stat[:, 0:1], statc[:, i, 0:1], -1.0,
            ALU.mult, ALU.mult), inc="po"))
        xb = [xnb2[par], xqb2a[par]]
        for h2 in (0, 1):
            buf = (2 * i + h2) % 3
            astate["half_done"][2 * i + h2] = ("po", sc.emit(
                "gpsimd", lambda g, buf=buf, h2=h2, i=i, xb=xb: g.tensor_scalar(
                    xb[h2][:], xst[:, 512 * buf:512 * buf + 512],
                    statc[:, i, 0:1], statc[:, i, 1:2], ALU.mult, ALU.add),
                waits=(c_nmr, ("pe", dep["xnb_rd"][par])), inc="po"))
            astate["slot_last"][buf] = astate["half_done"][2 * i + h2]
        c_xb = ("po", astate["half_done"][2 * i + 1][1])
        if dep["a1_scatter"] is not None:
            dep["a1_scatter"]()
            dep["a1_scatter"] = None
        c_tr = None
        for h2 in (0, 1):
            for j in range(4):
                c_tr = sc.emit("tensor", lambda t, h2=h2, j=j, xb=xb: t.transpose(
                    pt[:, 512 * h2 + 128 * j:512 * h2 + 128 * j + 128],
                    xb[h2][:, 128 * j:128 * j + 128], idb[:]),
                    waits=(c_xb, dep["pt_free"],
                           ("ldw", dep["w_small"])) if (h2 == 0 and j == 0) else (),
                    inc="pe" if (h2 == 1 and j == 3) else None)
        dep["pt_user"] = c_tr
        dep["xnb_rd"][par] = c_tr
        c_ev = sc.emit("scalar", lambda s, xK=xK: s.activation(
            xK[:].rearrange("p k f -> p (k f)"), pt[:], AF.Copy),
            waits=(("pe", c_tr), ("pe", dep["xnK_rd"][par]),
                   ("po", dep["xnK_rd2"][par])), inc="ac")
        dep["xnK_wr"][par] = c_ev
        dep["pt_free"] = ("ac", c_ev)
        G = S // 64
        m_lo, m_hi = 2 * i, 2 * i + 1
        c_mm = None
        for kk in range(8):
            w = (("ac", c_ev), ("wcw", dep["w_qkv"]),
                 ("dv", dep["slot_free"][2 * par])) if kk == 0 else ()
            c_mm = sc.emit("tensor", lambda t, kk=kk, xK=xK, slotA=slotA: t.matmul(
                slotA, xK[:, kk, :], Wqk[:, kk, 512:1024],
                start=(kk == 0), stop=(kk == 7)), waits=w, inc="pe" if kk == 7 else None)
        dep["xnK_rd"][par] = c_mm
        c_kt = sc.emit("scalar", lambda s, ktmp=ktmp, slotA=slotA: s.activation(
            ktmp[:], slotA, AF.Copy), waits=(("pe", c_mm),), inc="ac")
        c_sw = sc.emit("tensor", lambda t, ktmp=ktmp, slotB=slotB: t.matmul(
            slotB, psw[:], ktmp[:], start=True, stop=True),
            waits=(("dv", dep["slot_free_b"][par]), ("ac", c_kt)),
            inc="pe")
        dep["ktmp_rd"][par] = c_sw

        def scatter(par=par, m_lo=m_lo, c_sw=c_sw, c_kt=c_kt):
            full = sT[par][:]
            kv = kTa[:, :, :].rearrange("p q (d m) -> p q d m", m=G)
            src = full.rearrange("p (sl h e) -> p h e sl", sl=2, h=4)
            srcL = full.rearrange("p (sl he) -> p sl he", sl=2) \
                       .rearrange("p sl (h e) -> p sl h e", h=8)[0:64, :, 0::2, :] \
                       .transpose([0, 2, 3, 1])
            srcU = full.rearrange("p (sl he) -> p sl he", sl=2) \
                       .rearrange("p sl (h e) -> p sl h e", h=8)[64:128, :, 1::2, :] \
                       .transpose([0, 2, 3, 1])[:, :, :, ::-1]
            c_e2 = sc.emit("vector", lambda v: v.tensor_copy(
                kv[0:64, :, :, m_lo:m_lo + 2], srcL),
                waits=(("pe", c_sw), ("ac", c_kt)), inc="dv")
            dep["slot_free"][2 * par] = c_e2
            c_e4 = sc.emit("vector", lambda v: v.tensor_copy(
                kv[64:128, :, :, m_lo:m_lo + 2], srcU), inc="dv")
            dep["slot_free_b"][par] = c_e4
        dep["a1_scatter"] = scatter

    NMERGE = min(4, NTC)

    def a2k(i):
        return ("d", i % 3) if i < 4 * NMERGE else ("o", i % 4)

    def a2_xn(i, h2):
        par = i % 4
        if par < 2:
            return xnK[par][:].rearrange("p k f -> p (k f)")[:, 512 * h2:512 * h2 + 512]
        return Wqk[:, 2 * (par - 2) + h2, 512:1024]

    def a2_xq(i, h2):
        if i < 4 * NMERGE:
            return a2xq_d[:, i % 3, 512 * h2:512 * h2 + 512]
        par = i % 4
        return [ktmp2[0][:], ktmp2[1][:], w8n[:, 0:512], w8n[:, 512:1024],
                Wqk[:, 4, 512:1024], Wqk[:, 5, 512:1024],
                Wqk[:, 6, 512:1024], Wqk[:, 7, 512:1024]][2 * par + h2]

    dep["t_g_rd"] = 0

    def emit_a2_affine(i, merged):
        txn = [t_xn, t_xnB, t_xnC][i % 3]
        c_x32 = None
        for h2 in (0, 1):
            h = (2 * i + h2) if merged else (2 * NT + 2 * i + h2)
            buf = h % 3
            w = [("dv", dep["t_xn_free"][i % 3])]
            if not merged:
                w.append(astate["xdma"][h])
            c_x32 = sc.emit(
                "gpsimd", lambda g, buf=buf, h2=h2, i=i, txn=txn: g.tensor_scalar(
                    txn[:, 512 * h2:512 * h2 + 512], xst[:, 512 * buf:512 * buf + 512],
                    statc[:, i, 0:1], statc[:, i, 1:2], ALU.mult, ALU.add),
                waits=tuple(w), inc="po")
            astate["slot_last"][buf] = ("po", c_x32)
        dep["a2_x32"] = ("po", c_x32)
        dep["a2aff"][i] = c_x32

    def emit_tile_a2_stats(i, merged=False):
        stat = stat2[i % 2]
        txn = [t_xn, t_xnB, t_xnC][i % 3]
        if not merged:
            emit_a2_affine(i, False)
            return
        c_x32 = dep["a2aff"][i]
        sc.emit("vector", lambda v, stat=stat, txn=txn: v.reduce_max(
            stat[:, 4:5], txn[:], AX.X, apply_absolute_value=True),
            waits=(("po", c_x32),), inc="dv")
        sc.emit("vector", lambda v, stat=stat: v.tensor_scalar_max(
            stat[:, 4:5], stat[:, 4:5], EPS_Q), inc="dv")
        sc.emit("vector", lambda v, stat=stat: v.reciprocal(stat[:, 5:6], stat[:, 4:5]),
                inc="dv")
        sc.emit("vector", lambda v, stat=stat: v.tensor_scalar_mul(
            stat[:, 5:6], stat[:, 5:6], 127.0), inc="dv")
        sc.emit("vector", lambda v, stat=stat: v.tensor_scalar_mul(
            stat[:, 6:7], stat[:, 4:5], -INV127), inc="dv")
        dep["negx_last"] = ("dv", sc.emit("vector", lambda v, i=i, stat=stat:
            v.tensor_copy(negx[:, i:i + 1], stat[:, 6:7]),
            waits=(("pe", dep["negx_rd"]),), inc="dv"))

    def emit_prestats(i):
        stat = stat2[i % 2]
        txn = [t_xn, t_xnB, t_xnC][i % 3]
        c_x32 = dep["a2aff"][i]
        sc.emit("vector", lambda v, stat=stat, txn=txn: v.reduce_max(
            stat[:, 4:5], txn[:], AX.X, apply_absolute_value=True),
            waits=(("po", c_x32),), inc="dv")
        sc.emit("vector", lambda v, stat=stat: v.tensor_scalar_max(
            stat[:, 4:5], stat[:, 4:5], EPS_Q), inc="dv")
        sc.emit("vector", lambda v, stat=stat: v.reciprocal(stat[:, 5:6], stat[:, 4:5]),
                inc="dv")
        sc.emit("vector", lambda v, i=i, stat=stat: v.tensor_scalar_mul(
            statq[:, i, 0:1], stat[:, 5:6], 127.0), inc="dv")
        sc.emit("vector", lambda v, i=i, stat=stat: v.tensor_scalar_mul(
            negx[:, i:i + 1], stat[:, 4:5], -INV127), inc="dv")
        dep["t_xn_free"][i % 3] = sc.cnt["dv"]
        dep["negx_last"] = ("dv", sc.cnt["dv"])

    def emit_tile_a2_quant(i, halves=(0, 1), parts=(0, 1)):
        stat = stat2[i % 2]
        txn = [t_xn, t_xnB, t_xnC][i % 3]
        dep.setdefault("a2qpo", {})
        for h2 in halves:
            off = 512 * h2
            if 0 in parts:
                if i >= 4 * NMERGE:
                    w = [("pe", dep["a2buf_rd"].get(a2k(i), 0)),
                         ("pe", dep["xnK_rd"][i % 2]), ("pe", dep["ktmp_rd"][i % 2])]
                    sc.emit("gpsimd", lambda g, off=off, i=i, h2=h2, txn=txn:
                            g.tensor_copy(a2_xn(i, h2), txn[:, off:off + 512]),
                            waits=tuple(w), inc="po")
                inv = stat[:, 5:6] if i < 4 * NMERGE else statq[:, i, 0:1]
                if i < 4 * NMERGE:
                    wg = (("pe", dep["a2buf_rd"].get(a2k(i), 0)),)
                else:
                    wg = (dep["a2_x32"],) if h2 == 0 else ()
                sc.emit("vector", lambda v, off=off, inv=inv, txn=txn:
                        v.tensor_scalar_mul(
                    t_g[:], txn[:, off:off + 512], inv), waits=wg, inc="dv")
                sc.emit("vector", lambda v, off=off, inv=inv, txn=txn:
                        v.scalar_tensor_tensor(
                    t_r[:], txn[:, off:off + 512], inv, t_g[:],
                    ALU.mult, ALU.subtract), inc="dv")
            if 1 in parts:
                sc.emit("vector", lambda v: v.scalar_tensor_tensor(
                    t_b[:], t_r[:], 0.5, t_g[:], ALU.is_ge, ALU.add),
                    waits=(("ac", dep["g1_free"]), ("po", dep["z_po"])), inc="dv")
                sc.emit("vector", lambda v, i=i, h2=h2: v.scalar_tensor_tensor(
                    a2_xq(i, h2), t_r[:], -0.5, t_b[:], ALU.is_le, ALU.subtract),
                    inc="dv")
                dep["a2q_h"].setdefault(i, {})[h2] = sc.cnt["dv"]
                dep["a2qpo_h"].setdefault(i, {})[h2] = sc.cnt["po"]
        dep["a2qpo"][i] = sc.cnt["po"]
        dep["t_xn_free"][i % 3] = sc.cnt["dv"]
        dep["t_b_rd"] = sc.cnt["dv"]
        return sc.cnt["dv"]

    def emit_tile_a2_pe(i):
        c_q = dep["a2q"][i]
        c_qpo = dep["a2qpo"][i]
        tcol = (i % 4) * 128
        c_tr = None
        if i < 4 * NMERGE:
            for h2 in (0, 1):
                for j in range(4):
                    c_tr = sc.emit("tensor", lambda t, j=j, h2=h2, i=i: t.transpose(
                        pt[:, 512 * h2 + 128 * j:512 * h2 + 128 * j + 128],
                        a2_xq(i, h2)[:, 128 * j:128 * j + 128], idb[:]),
                        waits=(("dv", c_q), ("po", c_qpo), dep["pt_free"])
                        if (h2 == 0 and j == 0) else (),
                        inc="pe" if (h2 == 1 and j == 3) else None)
            dep["pt_user"] = c_tr
            c_ev = None
            for k in range(8):
                c_ev = sc.emit("scalar", lambda s, k=k, tcol=tcol: s.activation(
                    xnq[:, 8 + k, tcol:tcol + 128],
                    pt[:, 128 * k:128 * k + 128], AF.Copy),
                    waits=(("pe", c_tr),) if k == 0 else (),
                    inc="ac" if k == 7 else None)
            dep["pt_free"] = ("ac", c_ev)
        else:
            for h2 in (0, 1):
                emit_tile_a2_pe_h(i, h2)
            return
        dep["a2buf_rd"][a2k(i)] = dep["pt_user"]

    def emit_tile_a2_pe_h(i, h2):
        c_q = dep["a2q_h"][i][h2]
        c_qpo = dep["a2qpo_h"][i][h2]
        tcol = (i % 4) * 128
        c_tr = None
        for j in range(4):
            c_tr = sc.emit("tensor", lambda t, j=j, h2=h2, i=i: t.transpose(
                pt[:, 128 * j:128 * j + 128],
                a2_xn(i, h2)[:, 128 * j:128 * j + 128],
                idb[:]),
                waits=(("dv", c_q), ("po", c_qpo), dep["pt_free"]) if j == 0 else ())
            c_tr = sc.emit("tensor", lambda t, j=j, h2=h2, i=i: t.transpose(
                pt[:, 512 + 128 * j:512 + 128 * j + 128],
                a2_xq(i, h2)[:, 128 * j:128 * j + 128], idb[:]),
                inc="pe" if j == 3 else None)
        dep["pt_user"] = c_tr
        c_ev = sc.emit("vector", lambda v, h2=h2, tcol=tcol: v.tensor_copy(
            xnq[:, :, tcol:tcol + 128]
                .rearrange("p (g k) f -> p g k f", g=2)[:, :, 4 * h2:4 * h2 + 4, :],
            pt[:].rearrange("p (g k f) -> p g k f", g=2, k=4)),
            waits=(("pe", c_tr),), inc="dv")
        dep["pt_free"] = ("dv", c_ev)
        if h2 == 1:
            dep["a2buf_rd"][a2k(i)] = dep["pt_user"]

    def emit_xn_copy(i):
        par = i % 2
        tcol = (i % 4) * 128
        c_cp = sc.emit("gpsimd", lambda g, par=par, tcol=tcol: g.tensor_copy(
            xnq[:, 0:8, tcol:tcol + 128], xnK[par][:]),
            waits=(("ac", dep["xnK_wr"][par]), ("pe", dep["xnq_rd"])), inc="po")
        dep["xnK_rd2"][par] = c_cp

    def emit_negxsc_row(tc):
        c_mm = None
        for j in range(4):
            sl = 4 * tc + j
            c_mm = sc.emit("tensor", lambda t, j=j, sl=sl: t.matmul(
                pm[0:1, 128 * j:128 * j + 128], negx[:, sl:sl + 1], idf[:],
                start=True, stop=True),
                waits=(dep["negx_last"], dep["pm_free"],
                       ("pe", dep["rows0_rd"]), ("ldw", dep["w_small"])) if j == 0 else (),
                inc="pe" if j == 3 else None)
        c_rv = sc.emit("vector", lambda v: v.tensor_copy(rows[0:1, 0:512], pm[0:1, 0:512]),
                       waits=(("pe", c_mm),), inc="dv")
        dep["pm_free"] = ("dv", c_rv)
        dep["negxsc_row"] = c_rv
        dep["negx_rd"] = c_mm

    def emit_qkv_f(tc, f, half):
        xr = dep["xnT_ready"][tc]
        c_mm = None
        for kk in range(4 * half, 4 * half + 4):
            w = ()
            if kk == 0:
                w = [("dv", xr), ("po", dep["xnT_po"][tc]),
                     ("ac", dep["xnT_ac"][tc]),
                     ("wcw", dep["w_qkv"]), dep["pm_free"],
                     ("pe", dep["att_q_free"][f][tc % 3])]
            c_mm = sc.emit("tensor", lambda t, kk=kk, f=f: t.matmul(
                pm[:], Wqk[:, kk, 128 * f:128 * f + 128], xnq[:, kk, :],
                start=(kk == 0), stop=(kk == 7)), waits=w, inc="pe" if kk == 7 else None)
        if half == 0:
            return
        dep["xnq_rd"] = c_mm
        c_ev = sc.emit("scalar", lambda s, f=f, tc=tc: s.activation(
            qT[f][:, (tc % 3) * TC:(tc % 3) * TC + TC], pm[:], AF.Copy, scale=0.125),
            waits=(("pe", c_mm),), inc="ac")
        dep["pm_free"] = ("ac", c_ev)

    def emit_v_sub(tc, vs, half):
        xr = dep["xnT_ready"][tc]
        c_mm = None
        for kk in range(4 * half, 4 * half + 4):
            w = (("dv", xr), ("po", dep["xnT_po"][tc]), ("ac", dep["xnT_ac"][tc]),
                 ("wcw", dep["w_qkv"]), dep["pm_free"]) if kk == 0 else ()
            c_mm = sc.emit("tensor", lambda t, kk=kk, vs=vs: t.matmul(
                pm[:], xnq[:, kk, 128 * vs:128 * vs + 128], Wv[:, kk, :],
                start=(kk == 0), stop=(kk == 7)), waits=w, inc="pe" if kk == 7 else None)
        if half == 0:
            return
        dep["xnq_rd"] = c_mm
        jb = 4 * tc + vs
        c_ev = sc.emit("vector", lambda v, jb=jb: v.tensor_copy(
            v_aug[:, jb, :].rearrange("p (h e) -> p h e", h=8)[:, :, 0:64],
            pm[:].rearrange("p (h e) -> p h e", h=8)),
            waits=(("pe", c_mm), ("dv", dep["v_ones"])), inc="dv")
        dep["pm_free"] = ("dv", c_ev)

    def emit_ffn_pre(tc):
        c_mm = sc.emit("tensor", lambda t: t.matmul(
            pm[:], rows[0:1, 512:640], rows[0:1, 0:512], start=True, stop=True),
            waits=(("dv", max(dep["negxsc_row"], dep["ones_r"])), dep["pm_free"],
                   ("ldw", dep["w_small"])), inc="pe")
        dep["rows0_rd"] = c_mm
        c_cp = sc.emit("scalar", lambda s: s.activation(xscBs[:], pm[:], AF.Copy),
                       waits=(("pe", c_mm),), inc="ac")
        dep["pm_free"] = ("ac", c_cp)
        dep["xscB_ready"] = ("ac", c_cp)

    def emit_ffn_oc(tc, oc, half):
        xr = dep["xqT_ready"][tc]
        c_mm = None
        for kk in range(4 * half, 4 * half + 4):
            w = (("dv", xr), ("ac", dep["xnT_ac"][tc]), ("dv", dep["w_qT"]),
                 dep["pm_free"], dep["xscB_ready"]) if kk == 0 else ()
            c_mm = sc.emit("tensor", lambda t, kk=kk, oc=oc: t.matmul(
                pm[:], wqT[:, kk, 128 * oc:128 * oc + 128], xnq[:, 8 + kk, :],
                start=(kk == 0), stop=(kk == 7)), waits=w, inc="pe" if kk == 7 else None)
        if half == 0:
            return
        c_g1 = sc.emit("vector", lambda v: v.tensor_tensor(
            g1s[:], pm[:], xscBs[:], ALU.mult),
            waits=(("pe", c_mm), ("ac", dep["g1_free"])), inc="dv")
        dep["pm_free"] = ("dv", c_g1)
        c_z = sc.emit("gpsimd", lambda g, oc=oc: g.tensor_scalar(
            t_b[:], g1s[:], wscb[:, oc:oc + 1], biab[:, oc:oc + 1],
            ALU.mult, ALU.add),
            waits=(("dv", max(c_g1, dep["t_b_rd"])), ("ac", dep["g1_free"])),
            inc="po")
        dep["z_po"] = c_z
        c_th = sc.emit("scalar", lambda s: s.activation(
            g1s[:], t_b[:], AF.Tanh, scale=0.5),
            waits=(("po", c_z), ("ldw", dep["w_small"])), inc="ac")
        dep["g1_free"] = c_th
        gsl = TC * (tc % 4)
        c_si = sc.emit("vector", lambda v, oc=oc, gsl=gsl: v.scalar_tensor_tensor(
            gT[:, oc, gsl:gsl + TC], g1s[:], 1.0, t_b[:],
            ALU.add, ALU.mult),
            waits=(("ac", c_th), ("dv", dep["gT_rd"][tc % 4])), inc="dv")
        dep["t_b_rd"] = c_si
        return c_si

    def qq(i):
        dep["a2q"][i] = emit_tile_a2_quant(i)

    def qq2(i):
        dep["a2q"][i] = emit_tile_a2_quant(i, (1,))

    def a_rest_steps(tc):
        steps = []
        for j in range(4):
            if tc < NMERGE:
                steps.append((lambda i=4 * tc + j: emit_xn_copy(i), None,
                              4 * tc + j + 1))
                steps.append((lambda i=4 * tc + j: emit_tile_a2_pe(i), None,
                              4 * tc + j + 2))
            else:
                steps.append((lambda i=4 * tc + j: emit_tile_a2_pe_h(i, 0), None, 0))
                steps.append((lambda i=4 * tc + j: emit_tile_a2_pe_h(i, 1), None, 0))

        def fin_tiles(tc=tc):
            dep["xnT_ready"][tc] = sc.cnt["dv"]
            dep["xqT_ready"][tc] = sc.cnt["dv"]
            dep["xnT_po"][tc] = sc.cnt["po"]
            dep["xnT_ac"][tc] = sc.cnt["ac"]
            emit_negxsc_row(tc)
        steps.append((fin_tiles, f"tilesdone{tc}", 4 * tc + 5))
        for vs in range(4):
            g = 4 * tc + 5
            steps.append((lambda vs=vs, tc=tc: emit_v_sub(tc, vs, 0), "open", g))
            steps.append((lambda vs=vs, tc=tc: emit_v_sub(tc, vs, 1), None, g))

        def fin_v(tc=tc):
            dep["v"][tc] = sc.cnt["dv"]
        steps.append((fin_v, f"vdone{tc}", 0))
        steps.append((lambda tc=tc: emit_ffn_pre(tc), None, 0))
        for oc in range(4):
            g = 4 * tc + 5
            steps.append((lambda oc=oc, tc=tc: emit_ffn_oc(tc, oc, 0), "open", g))
            steps.append((lambda oc=oc, tc=tc: emit_ffn_oc(tc, oc, 1), None, g))

        def fin_g(tc=tc):
            dep["g"][tc] = sc.cnt["dv"]
        steps.append((fin_g, None, 0))
        return steps

    def a_q_steps(tc):
        steps = []
        for f in range(4):
            steps.append((lambda f=f, tc=tc: emit_qkv_f(tc, f, 0), "open",
                          4 * tc + 5))
            steps.append((lambda f=f, tc=tc: emit_qkv_f(tc, f, 1), None,
                          4 * tc + 5))

        def fin_qk(tc=tc):
            dep["qkT"][tc] = sc.cnt["ac"]
        steps.append((fin_qk, f"qkdone{tc}", 0))
        return steps

    def a_substeps(tc):
        steps = []
        tiles = [4 * tc + j for j in range(4)]

        def dma_pair(i):
            emit_x_dma(2 * NT + 2 * i)
            emit_x_dma(2 * NT + 2 * i + 1)

        def sd(j):
            i = tiles[j]
            dma_pair(i)
            emit_tile_a2_stats(i)
        rest = [(fn, tag) if tag else fn for fn, tag, _g in a_rest_steps(tc)]
        pes, rest = rest[:8], rest[8:]
        order = [("s", 0), ("qa", 0), ("qb", 0), ("s", 1), ("qa", 1),
                 ("p", 0), ("qb", 1), ("p", 1), ("s", 2), ("qa", 2),
                 ("p", 2), ("qb", 2), ("s", 3), ("p", 3), ("qa", 3),
                 ("p", 4), ("qb", 3), ("p", 5), ("p", 6), ("p", 7)]
        for kind, j in order:
            if kind == "s":
                steps.append(lambda j=j: sd(j))
            elif kind == "qa":
                steps.append(lambda i=tiles[j]: emit_tile_a2_quant(i, (0,), (0,)))
                steps.append(lambda i=tiles[j]: emit_tile_a2_quant(i, (0,), (1,)))
            elif kind == "qb":
                steps.append(lambda i=tiles[j]: emit_tile_a2_quant(i, (1,), (0,)))
                steps.append(lambda i=tiles[j]: emit_tile_a2_quant(i, (1,), (1,)))
            else:
                steps.append(pes[j])
        for st in rest:
            steps.append(st)
        stepsB = [(fn, tag) if tag else fn for fn, tag, _g in a_q_steps(tc)]
        return steps, stepsB

    def emit_attention_chunk(c, pump, close_open, drain_pre, drain_full):
        for P in range(NPAIR):
            if P == 1:
                drain_full()
            njb = 4 * c + 4
            c_sc_last = None
            pend_av = []
            last_av = [None]

            def flush_av():
                if pend_av:
                    pend_av.pop(0)()

            pops_at = (0, 1, 2, 3) if njb <= 4 else (0, 2, 3, 4)
            for jb in range(njb):
                if P == 0 and jb == 4 * c - 2:
                    drain_pre()
                if att["pend_gate"] and jb in pops_at:
                    close_open()
                    att["pend_gate"].pop(0)()
                if jb >= 3:
                    flush_av()
                par = att["unit"] % 2
                pe3 = att["unit"] % 3
                o = max(0, 128 * jb - TC * c)
                w = TC - o
                is_diag = jb >= 4 * c
                qb = (c % 3) * TC
                w0 = (("ac", max(att["sT_rd"][par], dep["qkT"][c], dep["k_all_ac"])),
                      ("dv", dep["k_all"]))
                if not is_diag:
                    sc.emit("tensor", lambda t, P=P, jb=jb, o=o, w=w, par=par, qb=qb:
                            t.matmul(
                        sT[par][:, o:o + w],
                        kTa[0:64, P, 128 * jb:128 * jb + 128],
                        qT[P][0:64, qb + o:qb + o + w],
                        start=True, stop=True), waits=w0)
                    c_sc = sc.emit("tensor", lambda t, P=P, jb=jb, o=o, w=w, par=par,
                                   qb=qb: t.matmul(
                        sT[par][:, 512 + o:512 + o + w],
                        kTa[64:128, P, 128 * jb:128 * jb + 128],
                        qT[P][64:128, qb + o:qb + o + w],
                        start=True, stop=True), inc="pe")
                else:
                    sc.emit("tensor", lambda t, P=P, jb=jb, o=o, par=par, qb=qb:
                            t.matmul(
                        sT[par][:, o:o + 128],
                        kTa[0:64, P, 128 * jb:128 * jb + 128],
                        qT[P][0:64, qb + o:qb + o + 128],
                        start=True, stop=False), waits=w0)
                    sc.emit("tensor", lambda t, o=o, par=par: t.matmul(
                        sT[par][:, o:o + 128], idb[:], nmk[:],
                        start=False, stop=True))
                    sc.emit("tensor", lambda t, P=P, jb=jb, o=o, par=par, qb=qb:
                            t.matmul(
                        sT[par][:, 512 + o:512 + o + 128],
                        kTa[64:128, P, 128 * jb:128 * jb + 128],
                        qT[P][64:128, qb + o:qb + o + 128],
                        start=True, stop=False))
                    c_sc = sc.emit("tensor", lambda t, o=o, par=par: t.matmul(
                        sT[par][:, 512 + o:512 + o + 128], idb[:], nmk[:],
                        start=False, stop=True), inc=None if w > 128 else "pe")
                    if w > 128:
                        sc.emit("tensor", lambda t, P=P, jb=jb, o=o, w=w, par=par,
                                qb=qb: t.matmul(
                            sT[par][:, o + 128:o + w],
                            kTa[0:64, P, 128 * jb:128 * jb + 128],
                            qT[P][0:64, qb + o + 128:qb + o + w],
                            start=True, stop=True))
                        c_sc = sc.emit("tensor", lambda t, P=P, jb=jb, o=o, w=w,
                                       par=par, qb=qb: t.matmul(
                            sT[par][:, 512 + o + 128:512 + o + w],
                            kTa[64:128, P, 128 * jb:128 * jb + 128],
                            qT[P][64:128, qb + o + 128:qb + o + w],
                            start=True, stop=True), inc="pe")
                c_sc_last = c_sc
                c_ex = sc.emit("scalar", lambda s, par=par, pe3=pe3, o=o, w=w:
                               s.activation(
                    eT[pe3][:].rearrange("p (b n) -> p b n", b=2)[:, :, o:o + w],
                    sT[par][:].rearrange("p (b n) -> p b n", b=2)[:, :, o:o + w],
                    AF.Exp),
                    waits=(("pe", c_sc), ("pe", att["eT_rd"][pe3])), inc="ac")
                att["sT_rd"][par] = c_ex
                avw = [("ac", c_ex), ("dv", dep["v"][jb // 4])]
                hA, hB = 2 * P, 2 * P + 1

                def mk_av(jb=jb, o=o, w=w, pe3=pe3, hA=hA, hB=hB, njb=njb, avw=tuple(avw)):
                    def go():
                        ws = list(avw)
                        if jb == 0:
                            ws.append(("dv", att["O_rd"]))
                        sc.emit("tensor", lambda t: t.matmul(
                            Oa[:, o:o + w], v_aug[:, jb, 65 * hA:65 * hA + 65],
                            eT[pe3][:, o:o + w],
                            start=(jb == 0), stop=(jb == njb - 1)), waits=ws)
                        c_av = sc.emit("tensor", lambda t: t.matmul(
                            Ob[:, o:o + w], v_aug[:, jb, 65 * hB:65 * hB + 65],
                            eT[pe3][:, 512 + o:512 + o + w],
                            start=(jb == 0), stop=(jb == njb - 1)), inc="pe")
                        att["eT_rd"][pe3] = c_av
                        last_av[0] = c_av
                    return go
                pend_av.append(mk_av())
                att["unit"] += 1
                pump()
            while pend_av:
                pend_av.pop(0)()
            dep["att_q_free"][P][c % 3] = c_sc_last
            gst = {}

            def ga1(c=c, P=P, av=last_av[0], gst=gst):
                gst["lw"] = emit_gating_a1(c, P, av)

            def ga2(c=c, P=P, gst=gst):
                gst["le"] = emit_gating_a2(c, P, gst["lw"][0])

            def gb1(c=c, P=P, gst=gst):
                gst["xr"] = emit_gating_b1(c, P)

            def gb2(c=c, P=P, gst=gst):
                emit_gating_b2(c, P, gst["le"], gst["xr"], gst["lw"][1])
            att["pend_gate"] += [ga1, ga2, gb1, gb2]

    def emit_gating_a1(c, P, c_av_last):
        sc.emit("scalar", lambda s: s.activation(
            rows[64:65, 0:512], Oa[64:65, :], AF.Copy),
            waits=(("pe", c_av_last), ("pe", dep["rows64_rd"])))
        c_l = sc.emit("scalar", lambda s: s.activation(
            rows[64:65, 512:1024], Ob[64:65, :], AF.Copy), inc="ac")
        gsl = TC * (c % 4)
        sc.emit("vector", lambda v, P=P, gsl=gsl: v.tensor_tensor(
            wpr[0:64, :], Oa[0:64, :], gT[0:64, P, gsl:gsl + TC], ALU.mult),
            waits=(("pe", c_av_last), ("dv", dep["g"][c]),
                   ("pe", dep["wpr_free"])))
        c_w = sc.emit("vector", lambda v, P=P, gsl=gsl: v.tensor_tensor(
            wpr[64:128, :], Ob[0:64, :], gT[64:128, P, gsl:gsl + TC], ALU.mult),
            inc="dv")
        att["O_rd"] = c_w
        if P == NPAIR - 1:
            dep["gT_rd"][c % 4] = c_w
        return c_l, c_w

    def emit_gating_a2(c, P, c_l):
        c_lt = None
        for it in range(4):
            c_lt = sc.emit("tensor", lambda t, it=it: t.matmul(
                pm[0:128, 2 * it:2 * it + 1], rows[64:65, 128 * it:128 * it + 128],
                idf[64:65, 64:65], start=True, stop=True),
                waits=(("ac", c_l), dep["pm_free"]) if it == 0 else ())
            c_lt = sc.emit("tensor", lambda t, it=it: t.matmul(
                pm[0:128, 2 * it + 1:2 * it + 2],
                rows[64:65, 512 + 128 * it:512 + 128 * it + 128],
                idf[64:65, 64:65], start=True, stop=True),
                inc="pe" if it == 3 else None)
        dep["rows64_rd"] = c_lt
        c_le = sc.emit("vector", lambda v: v.reciprocal(linvT[:, 0:8], pm[0:128, 0:8]),
                       waits=(("pe", c_lt),), inc="dv")
        dep["pm_free"] = ("dv", c_le)
        return c_le

    def emit_gating_b1(c, P):
        par = (c * NPAIR + P) % 2
        c_xr = sc.emit("gpsimd", lambda g, c=c, P=P, par=par: g.dma_start(
            out=xres[par][:].rearrange("p (it f) -> p it f", it=4),
            in_=xr_h[TC * c:TC * c + TC, :].rearrange("(it p) d -> p it d", p=128)
                 [:, :, 128 * P:128 * P + 128]),
            waits=(("dv", dep["xres_free"][par]),), inc=f"xr{par}", amt=16)
        return c_xr

    def emit_gating_b2(c, P, c_le, c_xr, c_w):
        par = (c * NPAIR + P) % 2
        c_wt = None
        for it in range(4):
            c_wt = sc.emit("tensor", lambda t, it=it: t.transpose(
                pt[:, 128 * it:128 * it + 128], wpr[:, 128 * it:128 * it + 128], idb[:]),
                waits=(("dv", c_w), dep["pt_free"]) if it == 0 else (),
                inc="pe" if it == 3 else None)
        dep["pt_user"] = c_wt
        dep["wpr_free"] = c_wt
        c_f = None
        for it in range(4):
            c_f = sc.emit("vector", lambda v, it=it, par=par: v.scalar_tensor_tensor(
                outs[par][:, 128 * it:128 * it + 64],
                pt[:, 128 * it:128 * it + 64], linvT[:, 2 * it:2 * it + 1],
                xres[par][:, 128 * it:128 * it + 64], ALU.mult, ALU.add),
                waits=(("pe", c_wt), (f"xr{par}", c_xr), ("dv", c_le),
                       (f"out{par}", dep["outs_free"][par])) if it == 0 else ())
            c_f = sc.emit("vector", lambda v, it=it, par=par: v.scalar_tensor_tensor(
                outs[par][:, 128 * it + 64:128 * it + 128],
                pt[:, 128 * it + 64:128 * it + 128], linvT[:, 2 * it + 1:2 * it + 2],
                xres[par][:, 128 * it + 64:128 * it + 128], ALU.mult, ALU.add),
                inc="dv" if it == 3 else None)
        dep["pt_free"] = ("dv", c_f)
        dep["xres_free"][par] = c_f
        c_out = sc.emit("gpsimd", lambda g, c=c, P=P, par=par: g.dma_start(
            out=out_h[TC * c:TC * c + TC, :].rearrange("(it p) d -> p it d", p=128)
                 [:, :, 128 * P:128 * P + 128],
            in_=outs[par][:].rearrange("p (it f) -> p it f", it=4)),
            waits=(("dv", c_f),), inc=f"out{par}", amt=16)
        dep["outs_free"][par] = c_out

    emit_weights()
    M = []
    for tc in range(NMERGE):
        M += a_rest_steps(tc)
        if tc < 3:
            M += a_q_steps(tc)
    mst = {"k": 0}

    def pump_merged(i, drain=False):
        target = len(M) if drain else min(
            len(M), (max(0, i - 2) * len(M)) // max(1, NT + 6))
        while mst["k"] < len(M):
            fn, tag, gate = M[mst["k"]]
            tag = tag if tag == "open" else None
            if not drain and gate > i - 1 and not (mst["k"] < target and gate <= i):
                break
            fn()
            mst["k"] += 1
            while mst["k"] < len(M) and M[mst["k"] - 1][1] == "open":
                M[mst["k"]][0]()
                mst["k"] += 1

    emit_x_dma(0)
    emit_x_dma(1)
    emit_x_dma(2)
    emit_a1_stats(0)
    for i in range(NT):
        pump_merged(i)
        emit_tile_a1(i)
        emit_a2_affine(i, True)
        for h in (2 * i + 3, 2 * i + 4):
            if h < 2 * NT:
                emit_x_dma(h)
        if i + 1 < NT:
            emit_a1_stats(i + 1)
        if 1 <= i and i - 1 < 4 * NMERGE:
            qq(i - 1)
        if i < 4 * NMERGE:
            emit_tile_a2_stats(i, merged=True)
        else:
            emit_prestats(i)
    for k in range(NT - 1, 4 * NMERGE):
        qq(k)
    if dep["a1_scatter"] is not None:
        dep["a1_scatter"]()
        dep["a1_scatter"] = None
    dep["k_all"] = sc.cnt["dv"]
    dep["k_all_ac"] = 0

    def norm(raw):
        return [e if isinstance(e, tuple) else (e, None) for e in raw]

    QA = []
    qa_end = {}
    flatA = []
    qb = {}

    qa_tiles_end = {}
    qa_pre_end = {}

    def append_tc(tc):
        sa, sb = a_substeps(tc)
        sa = norm(sa)
        for j, (fn, tag) in enumerate(sa):
            if tag and tag.startswith("tilesdone"):
                qa_tiles_end[tc] = len(flatA) + j + 1
                sa[j] = (fn, None)
            elif tag and tag.startswith("vdone"):
                qa_pre_end[tc] = len(flatA) + j + 1
                sa[j] = (fn, None)
        flatA.extend(sa)
        qa_end[tc] = len(flatA)
        qb[tc] = norm(sb)

    for tcq in range(3, NMERGE):
        if tcq < NTC:
            qb[tcq] = norm([(fn, tag) if tag else fn
                            for fn, tag, _g in a_q_steps(tcq)])
    qa_start = {}
    if mst["k"] < len(M):
        if NTC <= 2:
            pump_merged(NT, drain=True)
        else:
            for fn, tag, _g in M[mst["k"]:]:
                keep = tag if tag == "open" else None
                flatA.append((fn, keep))
                if tag and tag.startswith("qkdone"):
                    t = int(tag[6:])
                    qa_start[t] = qa_pre_end[t] = qa_end[t] = len(flatA)
                elif tag and tag.startswith("vdone"):
                    qa_pre_end[int(tag[5:])] = len(flatA)
                elif tag and tag.startswith("tilesdone"):
                    qa_tiles_end[int(tag[9:])] = len(flatA)
            mst["k"] = len(M)
            qa_end[min(NMERGE, NTC) - 1] = len(flatA)
    ia = {"k": 0}
    ib = {"k": 0, "lst": []}

    def stepA():
        fn, tag = flatA[ia["k"]]
        fn()
        ia["k"] += 1
        return tag

    def stepB():
        fn, tag = ib["lst"][ib["k"]]
        fn()
        ib["k"] += 1
        return tag

    def drain_open():
        while (ia["k"] > 0 and ia["k"] < len(flatA)
               and flatA[ia["k"] - 1][1] == "open"):
            stepA()
        while (ib["k"] > 0 and ib["k"] < len(ib["lst"])
               and ib["lst"][ib["k"] - 1][1] == "open"):
            stepB()

    for c in range(NTC):
        def drain_pre(c=c):
            while ia["k"] < qa_pre_end.get(c, qa_end.get(c, 0)):
                stepA()

        def drain_full(c=c):
            while ia["k"] < qa_end.get(c, 0):
                stepA()
        while ia["k"] < qa_start.get(c, 0):
            stepA()
        if c == 0:
            drain_pre()
        while ib["k"] < len(ib["lst"]):
            stepB()
        if NMERGE <= c + 2 < NTC:
            append_tc(c + 2)
        ib["lst"] = qb.get(c + 2, [])
        ib["k"] = 0
        ib["gate"] = qa_tiles_end.get(c + 2, 0)

        uc = {"u": 0}
        nunits = NPAIR * (4 * c + 4)
        dlA = 0

        def pump():
            uc["u"] += 1
            left = max(1, nunits - uc["u"] + 1)
            needed = max(0, dlA - ia["k"]) + (len(ib["lst"]) - ib["k"] + 1) // 2
            budget = max(1, -(-needed // left))
            for _ in range(budget):
                if ib["k"] < len(ib["lst"]) and ia["k"] >= ib["gate"]:
                    stepB()
                elif ia["k"] < len(flatA):
                    stepA()
                else:
                    break
            drain_open()

        def close_open():
            drain_open()
        emit_attention_chunk(c, pump, close_open, drain_pre, drain_full)
    while ia["k"] < len(flatA):
        stepA()
    while ib["k"] < len(ib["lst"]):
        stepB()

    while att["pend_gate"]:
        att["pend_gate"].pop(0)()
    sc.emit("gpsimd", lambda g: g.engine_nop(),
            waits=(("out0", sc.cnt["out0"]), ("out1", sc.cnt["out1"])))

    with nc.Block() as block:
        @block.sync
        def _(s):
            sc.replay("sync", s, sems)

        @block.gpsimd
        def _(g):
            sc.replay("gpsimd", g, sems)

        @block.vector
        def _(v):
            sc.replay("vector", v, sems)

        @block.scalar
        def _(s):
            sc.replay("scalar", s, sems)

        @block.tensor
        def _(t):
            sc.replay("tensor", t, sems)

    stack.close()
    return nc


_NC_CACHE = {}


def _get_nc(S):
    if S not in _NC_CACHE:
        _NC_CACHE[S] = build_program(S)
    return _NC_CACHE[S]


def kernel(L0_attn_x, L0_attn_qkv_weights, L0_ffn_weight_q, L0_ffn_weight_scale,
           L0_ffn_bias):
    x = np.asarray(L0_attn_x)
    W = np.asarray(L0_attn_qkv_weights)
    w8 = np.asarray(L0_ffn_weight_q)
    wsc = np.asarray(L0_ffn_weight_scale)
    bia = np.asarray(L0_ffn_bias)
    B, S, D_ = x.shape
    assert D_ == D

    ident_b = np.eye(128, dtype=ml_dtypes.bfloat16)
    ident_f = np.eye(128, dtype=np.float32)
    j = np.arange(128)
    maskT = (j[:, None] <= j[None, :]).astype(ml_dtypes.bfloat16)
    onesf = np.ones((128, 128), dtype=np.float32)
    pswap = np.zeros((128, 128), dtype=ml_dtypes.bfloat16)
    pswap[(np.arange(128) + 64) % 128, np.arange(128)] = 1
    negmask = np.where(j[:, None] > j[None, :], -1e9, 0.0).astype(ml_dtypes.bfloat16)

    nc = _get_nc(S)
    in_maps = []
    for c in range(8):
        b, half = c // 2, c % 2
        hs = DL * half
        in_maps.append({
            "x": np.ascontiguousarray(x[b]),
            "xr": np.ascontiguousarray(x[b][:, hs:hs + DL]),
            "wq": np.ascontiguousarray(W[:, hs:hs + DL]),
            "wk": np.ascontiguousarray(W[:, D + hs:D + hs + DL]),
            "wv": np.ascontiguousarray(W[:, 2 * D + hs:2 * D + hs + DL]),
            "w8": np.ascontiguousarray(w8[hs:hs + DL, :]),
            "wsc": np.ascontiguousarray(wsc[hs:hs + DL]),
            "bia": np.ascontiguousarray(bia[hs:hs + DL]),
            "idb": ident_b, "idf": ident_f,
            "psw": pswap, "nmk": negmask,
        })
    res = run_bass_kernel_spmd(nc, in_maps, core_ids=list(range(8)))
    out = np.empty((B, S, D), dtype=np.float32)
    for c in range(8):
        b, half = c // 2, c % 2
        out[b][:, DL * half:DL * half + DL] = np.asarray(res.results[c]["out"])
    return out


# version 57 (final)
# speedup vs baseline: 1.0256x; 1.0232x over previous
import numpy as np
import ml_dtypes
import concourse.bass as bass
import concourse.mybir as mybir
from concourse.bass_utils import run_bass_kernel_spmd

F32 = mybir.dt.float32
BF16 = mybir.dt.bfloat16
I8 = mybir.dt.int8
I32 = mybir.dt.int32
AF = mybir.ActivationFunctionType
ALU = mybir.AluOpType
AX = mybir.AxisListType

D = 1024
DL = 512
HD = 64
NPAIR = 4
TC = 512
EPS_LN = 1e-5
EPS_Q = 1e-6
INV127 = 1.0 / 127.0


class Sched:
    ENGS = ("sync", "gpsimd", "vector", "scalar", "tensor")

    def __init__(self):
        self.ops = {e: [] for e in self.ENGS}
        self.cnt = {s: 0 for s in (
            "ldx0", "ldx1", "ldx2", "ldx3", "ldx4", "ldx5", "ldw", "wcw", "wc8",
            "xr0", "xr1", "out0", "out1", "pe", "dv", "ac", "po")}

    OWN = {"vector": "dv", "scalar": "ac", "gpsimd": "po"}

    def emit(self, eng, fn, waits=(), inc=None, amt=1):
        if inc is None and eng in self.OWN:
            inc = self.OWN[eng]
        rec = None
        if inc is not None:
            self.cnt[inc] += amt
            rec = self.cnt[inc]
        ws = tuple((s, v) for (s, v) in waits if v)
        self.ops[eng].append((ws, fn, (inc, amt) if inc else None))
        return rec

    def replay(self, eng, handle, sems):
        seen = {}
        own = self.OWN.get(eng)
        prior = 0
        for waits, fn, inc in self.ops[eng]:
            allw = list(waits)
            if own is not None and prior > 0:
                allw.append((own, prior))
            for s, v in allw:
                if seen.get(s, 0) >= v:
                    continue
                handle.wait_ge(sems[s], v)
                seen[s] = v
            ins = fn(handle)
            if inc is not None:
                s, amt = inc
                ins.then_inc(sems[s], amt)
                if s == own:
                    prior += amt


def build_program(S=4096):
    NT = S // 128
    NTC = S // TC
    NJB = S // 128

    nc = bass.Bass("TRN2", target_bir_lowering=False)

    x_h = nc.dram_tensor("x", [S, D], F32, kind="ExternalInput").ap()
    xr_h = nc.dram_tensor("xr", [S, DL], F32, kind="ExternalInput").ap()
    wq_h = nc.dram_tensor("wq", [D, DL], F32, kind="ExternalInput").ap()
    wk_h = nc.dram_tensor("wk", [D, DL], F32, kind="ExternalInput").ap()
    wv_h = nc.dram_tensor("wv", [D, DL], F32, kind="ExternalInput").ap()
    w8_h = nc.dram_tensor("w8", [DL, D], I8, kind="ExternalInput").ap()
    wsc_h = nc.dram_tensor("wsc", [DL], F32, kind="ExternalInput").ap()
    bia_h = nc.dram_tensor("bia", [DL], F32, kind="ExternalInput").ap()
    idb_h = nc.dram_tensor("idb", [128, 128], BF16, kind="ExternalInput").ap()
    idf_h = nc.dram_tensor("idf", [128, 128], F32, kind="ExternalInput").ap()
    psw_h = nc.dram_tensor("psw", [128, 128], BF16, kind="ExternalInput").ap()
    nmk_h = nc.dram_tensor("nmk", [128, 128], BF16, kind="ExternalInput").ap()
    out_h = nc.dram_tensor("out", [S, DL], F32, kind="ExternalOutput").ap()

    from contextlib import ExitStack
    stack = ExitStack()
    sb = lambda name, shape, dt: stack.enter_context(nc.sbuf_tensor(name, shape, dt))
    ps = lambda name, shape, dt: stack.enter_context(nc.psum_tensor(name, shape, dt))

    qT = [sb(f"qT{p}", [128, 3 * TC], BF16) for p in range(NPAIR)]
    kTa = sb("kTa", [128, NPAIR, S], BF16)
    v_aug = sb("v_aug", [128, NJB, 520], BF16)
    gT = sb("gT", [128, 4, 4 * TC], BF16)
    t_xnB = sb("t_xnB", [128, 1024], F32)
    a2xq_d = sb("a2xq_d", [128, 3, 1024], BF16)
    t_xnC = sb("t_xnC", [128, 1024], F32)
    Wqk = sb("Wqk", [128, 8, 1024], BF16)
    Wv = sb("Wv", [128, 8, 512], BF16)
    wqT = sb("wqT", [128, 8, 512], BF16)
    w8n = sb("w8n", [128, 1024], BF16)
    wscb = sb("wscb", [128, 4], F32)
    biab = sb("biab", [128, 4], F32)
    idb = sb("idb_s", [128, 128], BF16)
    idf = sb("idf_s", [128, 128], F32)
    psw = sb("psw_s", [128, 128], BF16)
    nmk = sb("nmk_s", [128, 128], BF16)
    nhalf = sb("nhalf", [128, 1], F32)
    xst = sb("xst", [128, 3 * 512], F32)
    t_xn = sb("t_xn", [128, 1024], F32)
    t_b = sb("t_b", [128, 512], F32)
    t_g = sb("t_g", [128, 512], I32)
    t_r = sb("t_r", [128, 512], F32)
    xnbf = sb("xnbf", [128, 512], BF16)
    xqbf = sb("xqbf", [128, 512], BF16)
    xnb2 = [xnbf, sb("xnbf1", [128, 512], BF16)]
    xqb2a = [xqbf, sb("xqbf1", [128, 512], BF16)]
    xnq = sb("xnq", [128, 16, TC], BF16)
    g1s = sb("g1s", [128, 512], F32)
    xscBs = sb("xscBs", [128, 512], F32)
    rows = sb("rows", [128, 1024], F32)
    bnst = sb("bnst", [128, 12], F32)
    stat = sb("stat", [128, 8], F32)
    negx = sb("negx", [128, NT], F32)
    statq = sb("statq", [128, NT, 1], F32)
    statc = sb("statc", [128, NT, 2], F32)
    xnK = [sb(f"xnK{i}", [128, 8, 128], BF16) for i in range(2)]
    ktmp2 = [sb(f"ktmp{i}", [128, 512], BF16) for i in range(2)]
    bnst2 = [sb(f"bnst{i}", [128, 12], F32) for i in range(2)]
    stat2 = [sb(f"stat{i}", [128, 8], F32) for i in range(2)]
    eT = [sb(f"eT{i}", [128, 1024], BF16) for i in range(3)]
    wpr = sb("wpr", [128, 512], BF16)
    linvT = sb("linvT", [128, 8], F32)
    outs = [sb(f"outs{i}", [128, 512], F32) for i in range(2)]
    xres = [sb(f"xres{i}", [128, 512], F32) for i in range(2)]

    sT = [ps(f"sT{i}", [128, 1024], F32) for i in range(2)]
    Oa = ps("Oa", [65, 512], F32)
    Ob = ps("Ob", [65, 512], F32)
    pm = ps("pm", [128, 512], F32)
    pt = ps("pt", [128, 1024], BF16)

    sems = {n: stack.enter_context(nc.semaphore(n))
            for n in ("ldx0", "ldx1", "ldx2", "ldx3", "ldx4", "ldx5",
                      "ldw", "wcw", "wc8",
                      "xr0", "xr1", "out0", "out1", "pe", "dv", "ac", "po")}
    sc = Sched()

    dep = {
        "w_qkv": None, "w_qT": None, "w_small": None,
        "qkT": [None] * NTC, "v": [None] * NTC, "g": [None] * NTC,
        "xnT_po": [0] * NTC, "xnT_ac": [0] * NTC,
        "xnT_ready": [None] * NTC, "xqT_ready": [None] * NTC,
        "pm_free": ("dv", 0), "pt_free": ("dv", 0), "pt_user": 0,
        "v_ones": 0, "negxsc_row": 0, "xscB_ready": ("dv", 0),
        "rows0_rd": 0, "rows64_rd": 0, "g1_free": 0, "wpr_free": 0,
        "stat_free": [0, 0], "k_all": 0, "xnb_rd": [0, 0], "xnK_rd": [0, 0],
        "slot_free": [0, 0, 0, 0], "slot_free_b": [0, 0], "negx_last": 0,
        "a1_scatter": None, "ktmp_rd": [0, 0], "a2buf_rd": {}, "a2q": {}, "t_xn_free": [0, 0, 0],
        "xnK_rd2": [0, 0], "xnq_rd": 0, "xnK_wr": [0, 0],
        "gT_rd": [0, 0, 0, 0], "negx_rd": 0, "t_b_rd": 0, "z_po": 0, "a2_x32": ("po", 0),
        "c_st": {}, "a2aff": {}, "a2q_h": {}, "a2qpo_h": {}, "inv_po": 0,
        "att_q_free": [[0, 0, 0] for _ in range(NPAIR)],
        "xres_free": [0, 0], "outs_free": [0, 0],
    }
    astate = {"xdma": [None] * (4 * NT), "half_done": [None] * (4 * NT),
              "slot_last": {}}
    att = {"sT_rd": [0, 0], "eT_rd": [0, 0, 0], "O_rd": 0, "unit": 0,
           "pend_gate": []}

    def emit_weights():
        wq_r = wq_h.rearrange("(kk p) f -> p kk f", p=128)
        wk_r = wk_h.rearrange("(kk p) f -> p kk f", p=128)
        wv_r = wv_h.rearrange("(kk p) f -> p kk f", p=128)
        sc.emit("gpsimd", lambda g: g.dma_start(out=Wqk[:, :, 0:512], in_=wq_r),
                inc="wcw", amt=16)
        sc.emit("gpsimd", lambda g: g.dma_start(out=Wqk[:, :, 512:1024], in_=wk_r),
                inc="wcw", amt=16)
        dep["w_qkv"] = sc.emit("gpsimd", lambda g: g.dma_start(out=Wv[:], in_=wv_r),
                               inc="wcw", amt=16)
        for oc in range(4):
            sc.emit("sync", lambda s, oc=oc: s.dma_start(
                out=wscb[:, oc:oc + 1],
                in_=wsc_h[128 * oc:128 * oc + 128].rearrange("(p a) -> p a", a=1)),
                inc="ldw", amt=16)
            sc.emit("sync", lambda s, oc=oc: s.dma_start(
                out=biab[:, oc:oc + 1],
                in_=bia_h[128 * oc:128 * oc + 128].rearrange("(p a) -> p a", a=1)),
                inc="ldw", amt=16)
        sc.emit("sync", lambda s: s.dma_start(out=idb[:], in_=idb_h), inc="ldw", amt=16)
        sc.emit("sync", lambda s: s.dma_start(out=idf[:], in_=idf_h), inc="ldw", amt=16)
        sc.emit("sync", lambda s: s.dma_start(out=psw[:], in_=psw_h), inc="ldw", amt=16)
        dep["w_small"] = sc.emit("sync", lambda s: s.dma_start(out=nmk[:], in_=nmk_h),
                                 inc="ldw", amt=16)
        sc.emit("gpsimd", lambda g: g.memset(nhalf[:], -0.5), inc="po")
        dep["ones_r"] = sc.emit("vector", lambda v: v.memset(
            rows[0:1, 512:640], 1.0), inc="dv")
        dep["v_ones"] = sc.emit("vector", lambda v: v.memset(
            v_aug[:, :, 64:520:65], 2.0), inc="dv")
        last_ev = None
        for qt in range(4):
            w8q = w8_h[128 * qt:128 * qt + 128, :]
            c_ld = sc.emit("gpsimd", lambda g, w8q=w8q: g.dma_start(out=w8n[:], in_=w8q),
                           waits=(("dv", last_ev),) if last_ev else (), inc="wc8", amt=16)
            c_tr = None
            for kk in range(8):
                c_tr = sc.emit("tensor", lambda t, kk=kk: t.transpose(
                    pt[:, 128 * kk:128 * kk + 128], w8n[:, 128 * kk:128 * kk + 128], idb[:]),
                    waits=(("wc8", c_ld), ("ldw", dep["w_small"]),
                           dep["pt_free"]) if kk == 0 else (),
                    inc="pe" if kk == 7 else None)
            dep["pt_user"] = c_tr
            last_ev = sc.emit("vector", lambda v, qt=qt: v.tensor_copy(
                wqT[:, :, 128 * qt:128 * qt + 128],
                pt[:].rearrange("p (kk f) -> p kk f", kk=8)),
                waits=(("pe", c_tr),), inc="dv")
            dep["pt_free"] = ("dv", last_ev)
        dep["w_qT"] = last_ev

    def emit_x_dma(h):
        i = (h % (2 * NT)) // 2
        buf = h % 3
        prev = astate["slot_last"].get(buf)
        semn = f"ldx{h % 3}"
        cnt = sc.emit("sync", lambda s, i=i, h=h, buf=buf: s.dma_start(
            out=xst[:, 512 * buf:512 * buf + 512],
            in_=x_h[128 * i:128 * i + 128, 512 * (h % 2):512 * (h % 2) + 512]),
            waits=(prev,) if prev else (), inc=semn, amt=16)
        astate["xdma"][h] = (semn, cnt)

    def emit_a1_stats(i):
        par = i % 2
        bnst = bnst2[par]
        stat = stat2[par]
        for h2 in (0, 1):
            buf = (2 * i + h2) % 3
            sc.emit("vector", lambda v, buf=buf, h2=h2, bnst=bnst: v.bn_stats(
                bnst[:, 6 * h2:6 * h2 + 6], xst[:, 512 * buf:512 * buf + 512]),
                waits=(astate["xdma"][2 * i + h2],
                       ("po", dep["stat_free"][par])), inc="dv")
        sc.emit("vector", lambda v, bnst=bnst, stat=stat: v.bn_aggr(stat[:, 0:2], bnst[:]),
                inc="dv")
        dep["c_st"][i] = sc.cnt["dv"]

    def emit_tile_a1(i):
        par = i % 2
        bnst = bnst2[par]
        stat = stat2[par]
        xK = xnK[par]
        ktmp = ktmp2[par]
        slotA = sT[par][:, 0:512]
        slotB = sT[par][:, 512:1024]
        c_st = dep["c_st"][i]
        sc.emit("gpsimd", lambda g, stat=stat: g.tensor_scalar_add(
            stat[:, 7:8], stat[:, 1:2], EPS_LN), waits=(("dv", c_st),))
        c_sq = sc.emit("gpsimd", lambda g, stat=stat, i=i: g.tensor_tensor(
            statc[:, i, 0:1], stat[:, 7:8], nhalf[:], ALU.pow), inc="po")
        dep["stat_free"][par] = c_sq
        c_nmr = ("po", sc.emit("gpsimd", lambda g, i=i, stat=stat: g.tensor_scalar(
            statc[:, i, 1:2], stat[:, 0:1], statc[:, i, 0:1], -1.0,
            ALU.mult, ALU.mult), inc="po"))
        xb = [xnb2[par], xqb2a[par]]
        for h2 in (0, 1):
            buf = (2 * i + h2) % 3
            astate["half_done"][2 * i + h2] = ("po", sc.emit(
                "gpsimd", lambda g, buf=buf, h2=h2, i=i, xb=xb: g.tensor_scalar(
                    xb[h2][:], xst[:, 512 * buf:512 * buf + 512],
                    statc[:, i, 0:1], statc[:, i, 1:2], ALU.mult, ALU.add),
                waits=(c_nmr, ("pe", dep["xnb_rd"][par])), inc="po"))
            astate["slot_last"][buf] = astate["half_done"][2 * i + h2]
        c_xb = ("po", astate["half_done"][2 * i + 1][1])
        if dep["a1_scatter"] is not None:
            dep["a1_scatter"]()
            dep["a1_scatter"] = None
        c_tr = None
        for h2 in (0, 1):
            for j in range(4):
                c_tr = sc.emit("tensor", lambda t, h2=h2, j=j, xb=xb: t.transpose(
                    pt[:, 512 * h2 + 128 * j:512 * h2 + 128 * j + 128],
                    xb[h2][:, 128 * j:128 * j + 128], idb[:]),
                    waits=(c_xb, dep["pt_free"],
                           ("ldw", dep["w_small"])) if (h2 == 0 and j == 0) else (),
                    inc="pe" if (h2 == 1 and j == 3) else None)
        dep["pt_user"] = c_tr
        dep["xnb_rd"][par] = c_tr
        c_ev = sc.emit("scalar", lambda s, xK=xK: s.activation(
            xK[:].rearrange("p k f -> p (k f)"), pt[:], AF.Copy),
            waits=(("pe", c_tr), ("pe", dep["xnK_rd"][par]),
                   ("po", dep["xnK_rd2"][par])), inc="ac")
        dep["xnK_wr"][par] = c_ev
        dep["pt_free"] = ("ac", c_ev)
        G = S // 64
        m_lo, m_hi = 2 * i, 2 * i + 1
        c_mm = None
        for kk in range(8):
            w = (("ac", c_ev), ("wcw", dep["w_qkv"]),
                 ("dv", dep["slot_free"][2 * par])) if kk == 0 else ()
            c_mm = sc.emit("tensor", lambda t, kk=kk, xK=xK, slotA=slotA: t.matmul(
                slotA, xK[:, kk, :], Wqk[:, kk, 512:1024],
                start=(kk == 0), stop=(kk == 7)), waits=w, inc="pe" if kk == 7 else None)
        dep["xnK_rd"][par] = c_mm
        c_kt = sc.emit("scalar", lambda s, ktmp=ktmp, slotA=slotA: s.activation(
            ktmp[:], slotA, AF.Copy), waits=(("pe", c_mm),), inc="ac")
        c_sw = sc.emit("tensor", lambda t, ktmp=ktmp, slotB=slotB: t.matmul(
            slotB, psw[:], ktmp[:], start=True, stop=True),
            waits=(("dv", dep["slot_free_b"][par]), ("ac", c_kt)),
            inc="pe")
        dep["ktmp_rd"][par] = c_sw

        def scatter(par=par, m_lo=m_lo, c_sw=c_sw, c_kt=c_kt):
            full = sT[par][:]
            kv = kTa[:, :, :].rearrange("p q (d m) -> p q d m", m=G)
            src = full.rearrange("p (sl h e) -> p h e sl", sl=2, h=4)
            srcL = full.rearrange("p (sl he) -> p sl he", sl=2) \
                       .rearrange("p sl (h e) -> p sl h e", h=8)[0:64, :, 0::2, :] \
                       .transpose([0, 2, 3, 1])
            srcU = full.rearrange("p (sl he) -> p sl he", sl=2) \
                       .rearrange("p sl (h e) -> p sl h e", h=8)[64:128, :, 1::2, :] \
                       .transpose([0, 2, 3, 1])[:, :, :, ::-1]
            c_e2 = sc.emit("vector", lambda v: v.tensor_copy(
                kv[0:64, :, :, m_lo:m_lo + 2], srcL),
                waits=(("pe", c_sw), ("ac", c_kt)), inc="dv")
            dep["slot_free"][2 * par] = c_e2
            c_e4 = sc.emit("vector", lambda v: v.tensor_copy(
                kv[64:128, :, :, m_lo:m_lo + 2], srcU), inc="dv")
            dep["slot_free_b"][par] = c_e4
        dep["a1_scatter"] = scatter

    NMERGE = min(4, NTC)

    def a2k(i):
        return ("d", i % 3) if i < 4 * NMERGE else ("o", i % 4)

    def a2_xn(i, h2):
        par = i % 4
        if par < 2:
            return xnK[par][:].rearrange("p k f -> p (k f)")[:, 512 * h2:512 * h2 + 512]
        return Wqk[:, 2 * (par - 2) + h2, 512:1024]

    def a2_xq(i, h2):
        if i < 4 * NMERGE:
            return a2xq_d[:, i % 3, 512 * h2:512 * h2 + 512]
        par = i % 4
        return [ktmp2[0][:], ktmp2[1][:], w8n[:, 0:512], w8n[:, 512:1024],
                Wqk[:, 4, 512:1024], Wqk[:, 5, 512:1024],
                Wqk[:, 6, 512:1024], Wqk[:, 7, 512:1024]][2 * par + h2]

    dep["t_g_rd"] = 0

    def emit_a2_affine(i, merged):
        txn = [t_xn, t_xnB, t_xnC][i % 3]
        c_x32 = None
        for h2 in (0, 1):
            h = (2 * i + h2) if merged else (2 * NT + 2 * i + h2)
            buf = h % 3
            w = [("dv", dep["t_xn_free"][i % 3])]
            if not merged:
                w.append(astate["xdma"][h])
            c_x32 = sc.emit(
                "gpsimd", lambda g, buf=buf, h2=h2, i=i, txn=txn: g.tensor_scalar(
                    txn[:, 512 * h2:512 * h2 + 512], xst[:, 512 * buf:512 * buf + 512],
                    statc[:, i, 0:1], statc[:, i, 1:2], ALU.mult, ALU.add),
                waits=tuple(w), inc="po")
            astate["slot_last"][buf] = ("po", c_x32)
        dep["a2_x32"] = ("po", c_x32)
        dep["a2aff"][i] = c_x32

    def emit_tile_a2_stats(i, merged=False):
        stat = stat2[i % 2]
        txn = [t_xn, t_xnB, t_xnC][i % 3]
        if not merged:
            emit_a2_affine(i, False)
            return
        c_x32 = dep["a2aff"][i]
        sc.emit("vector", lambda v, stat=stat, txn=txn: v.reduce_max(
            stat[:, 4:5], txn[:], AX.X, apply_absolute_value=True),
            waits=(("po", c_x32),), inc="dv")
        sc.emit("vector", lambda v, stat=stat: v.tensor_scalar_max(
            stat[:, 4:5], stat[:, 4:5], EPS_Q), inc="dv")
        sc.emit("vector", lambda v, stat=stat: v.reciprocal(stat[:, 5:6], stat[:, 4:5]),
                inc="dv")
        sc.emit("vector", lambda v, stat=stat: v.tensor_scalar_mul(
            stat[:, 5:6], stat[:, 5:6], 127.0), inc="dv")
        sc.emit("vector", lambda v, stat=stat: v.tensor_scalar_mul(
            stat[:, 6:7], stat[:, 4:5], -INV127), inc="dv")
        dep["negx_last"] = ("dv", sc.emit("vector", lambda v, i=i, stat=stat:
            v.tensor_copy(negx[:, i:i + 1], stat[:, 6:7]),
            waits=(("pe", dep["negx_rd"]),), inc="dv"))

    def emit_prestats(i):
        stat = stat2[i % 2]
        txn = [t_xn, t_xnB, t_xnC][i % 3]
        c_x32 = dep["a2aff"][i]
        sc.emit("vector", lambda v, stat=stat, txn=txn: v.reduce_max(
            stat[:, 4:5], txn[:], AX.X, apply_absolute_value=True),
            waits=(("po", c_x32),), inc="dv")
        sc.emit("vector", lambda v, stat=stat: v.tensor_scalar_max(
            stat[:, 4:5], stat[:, 4:5], EPS_Q), inc="dv")
        sc.emit("vector", lambda v, stat=stat: v.reciprocal(stat[:, 5:6], stat[:, 4:5]),
                inc="dv")
        sc.emit("vector", lambda v, i=i, stat=stat: v.tensor_scalar_mul(
            statq[:, i, 0:1], stat[:, 5:6], 127.0), inc="dv")
        sc.emit("vector", lambda v, i=i, stat=stat: v.tensor_scalar_mul(
            negx[:, i:i + 1], stat[:, 4:5], -INV127), inc="dv")
        dep["t_xn_free"][i % 3] = sc.cnt["dv"]
        dep["negx_last"] = ("dv", sc.cnt["dv"])

    def emit_tile_a2_quant(i, halves=(0, 1), parts=(0, 1)):
        stat = stat2[i % 2]
        txn = [t_xn, t_xnB, t_xnC][i % 3]
        dep.setdefault("a2qpo", {})
        for h2 in halves:
            off = 512 * h2
            if 0 in parts:
                if i >= 4 * NMERGE:
                    w = [("pe", dep["a2buf_rd"].get(a2k(i), 0)),
                         ("pe", dep["xnK_rd"][i % 2]), ("pe", dep["ktmp_rd"][i % 2])]
                    sc.emit("gpsimd", lambda g, off=off, i=i, h2=h2, txn=txn:
                            g.tensor_copy(a2_xn(i, h2), txn[:, off:off + 512]),
                            waits=tuple(w), inc="po")
                inv = stat[:, 5:6] if i < 4 * NMERGE else statq[:, i, 0:1]
                if i < 4 * NMERGE:
                    wg = (("pe", dep["a2buf_rd"].get(a2k(i), 0)),)
                else:
                    wg = (dep["a2_x32"],) if h2 == 0 else ()
                sc.emit("vector", lambda v, off=off, inv=inv, txn=txn:
                        v.tensor_scalar_mul(
                    t_g[:], txn[:, off:off + 512], inv), waits=wg, inc="dv")
                sc.emit("vector", lambda v, off=off, inv=inv, txn=txn:
                        v.scalar_tensor_tensor(
                    t_r[:], txn[:, off:off + 512], inv, t_g[:],
                    ALU.mult, ALU.subtract), inc="dv")
            if 1 in parts:
                sc.emit("vector", lambda v: v.scalar_tensor_tensor(
                    t_b[:], t_r[:], 0.5, t_g[:], ALU.is_ge, ALU.add),
                    waits=(("ac", dep["g1_free"]), ("po", dep["z_po"])), inc="dv")
                sc.emit("vector", lambda v, i=i, h2=h2: v.scalar_tensor_tensor(
                    a2_xq(i, h2), t_r[:], -0.5, t_b[:], ALU.is_le, ALU.subtract),
                    inc="dv")
                dep["a2q_h"].setdefault(i, {})[h2] = sc.cnt["dv"]
                dep["a2qpo_h"].setdefault(i, {})[h2] = sc.cnt["po"]
        dep["a2qpo"][i] = sc.cnt["po"]
        dep["t_xn_free"][i % 3] = sc.cnt["dv"]
        dep["t_b_rd"] = sc.cnt["dv"]
        return sc.cnt["dv"]

    def emit_tile_a2_pe(i):
        c_q = dep["a2q"][i]
        c_qpo = dep["a2qpo"][i]
        tcol = (i % 4) * 128
        c_tr = None
        if i < 4 * NMERGE:
            for h2 in (0, 1):
                for j in range(4):
                    c_tr = sc.emit("tensor", lambda t, j=j, h2=h2, i=i: t.transpose(
                        pt[:, 512 * h2 + 128 * j:512 * h2 + 128 * j + 128],
                        a2_xq(i, h2)[:, 128 * j:128 * j + 128], idb[:]),
                        waits=(("dv", c_q), ("po", c_qpo), dep["pt_free"])
                        if (h2 == 0 and j == 0) else (),
                        inc="pe" if (h2 == 1 and j == 3) else None)
            dep["pt_user"] = c_tr
            c_ev = None
            for k in range(8):
                c_ev = sc.emit("scalar", lambda s, k=k, tcol=tcol: s.activation(
                    xnq[:, 8 + k, tcol:tcol + 128],
                    pt[:, 128 * k:128 * k + 128], AF.Copy),
                    waits=(("pe", c_tr),) if k == 0 else (),
                    inc="ac" if k == 7 else None)
            dep["pt_free"] = ("ac", c_ev)
        else:
            for h2 in (0, 1):
                emit_tile_a2_pe_h(i, h2)
            return
        dep["a2buf_rd"][a2k(i)] = dep["pt_user"]

    def emit_tile_a2_pe_h(i, h2):
        c_q = dep["a2q_h"][i][h2]
        c_qpo = dep["a2qpo_h"][i][h2]
        tcol = (i % 4) * 128
        c_tr = None
        for j in range(4):
            c_tr = sc.emit("tensor", lambda t, j=j, h2=h2, i=i: t.transpose(
                pt[:, 128 * j:128 * j + 128],
                a2_xn(i, h2)[:, 128 * j:128 * j + 128],
                idb[:]),
                waits=(("dv", c_q), ("po", c_qpo), dep["pt_free"]) if j == 0 else ())
            c_tr = sc.emit("tensor", lambda t, j=j, h2=h2, i=i: t.transpose(
                pt[:, 512 + 128 * j:512 + 128 * j + 128],
                a2_xq(i, h2)[:, 128 * j:128 * j + 128], idb[:]),
                inc="pe" if j == 3 else None)
        dep["pt_user"] = c_tr
        c_ev = sc.emit("vector", lambda v, h2=h2, tcol=tcol: v.tensor_copy(
            xnq[:, :, tcol:tcol + 128]
                .rearrange("p (g k) f -> p g k f", g=2)[:, :, 4 * h2:4 * h2 + 4, :],
            pt[:].rearrange("p (g k f) -> p g k f", g=2, k=4)),
            waits=(("pe", c_tr),), inc="dv")
        dep["pt_free"] = ("dv", c_ev)
        if h2 == 1:
            dep["a2buf_rd"][a2k(i)] = dep["pt_user"]

    def emit_xn_copy(i):
        par = i % 2
        tcol = (i % 4) * 128
        c_cp = sc.emit("gpsimd", lambda g, par=par, tcol=tcol: g.tensor_copy(
            xnq[:, 0:8, tcol:tcol + 128], xnK[par][:]),
            waits=(("ac", dep["xnK_wr"][par]), ("pe", dep["xnq_rd"])), inc="po")
        dep["xnK_rd2"][par] = c_cp

    def emit_negxsc_row(tc):
        c_mm = None
        for j in range(4):
            sl = 4 * tc + j
            c_mm = sc.emit("tensor", lambda t, j=j, sl=sl: t.matmul(
                pm[0:1, 128 * j:128 * j + 128], negx[:, sl:sl + 1], idf[:],
                start=True, stop=True),
                waits=(dep["negx_last"], dep["pm_free"],
                       ("pe", dep["rows0_rd"]), ("ldw", dep["w_small"])) if j == 0 else (),
                inc="pe" if j == 3 else None)
        c_rv = sc.emit("vector", lambda v: v.tensor_copy(rows[0:1, 0:512], pm[0:1, 0:512]),
                       waits=(("pe", c_mm),), inc="dv")
        dep["pm_free"] = ("dv", c_rv)
        dep["negxsc_row"] = c_rv
        dep["negx_rd"] = c_mm

    def emit_qkv_f(tc, f, half):
        xr = dep["xnT_ready"][tc]
        c_mm = None
        for kk in range(4 * half, 4 * half + 4):
            w = ()
            if kk == 0:
                w = [("dv", xr), ("po", dep["xnT_po"][tc]),
                     ("ac", dep["xnT_ac"][tc]),
                     ("wcw", dep["w_qkv"]), dep["pm_free"],
                     ("pe", dep["att_q_free"][f][tc % 3])]
            c_mm = sc.emit("tensor", lambda t, kk=kk, f=f: t.matmul(
                pm[:], Wqk[:, kk, 128 * f:128 * f + 128], xnq[:, kk, :],
                start=(kk == 0), stop=(kk == 7)), waits=w, inc="pe" if kk == 7 else None)
        if half == 0:
            return
        dep["xnq_rd"] = c_mm
        c_ev = sc.emit("scalar", lambda s, f=f, tc=tc: s.activation(
            qT[f][:, (tc % 3) * TC:(tc % 3) * TC + TC], pm[:], AF.Copy, scale=0.125),
            waits=(("pe", c_mm),), inc="ac")
        dep["pm_free"] = ("ac", c_ev)

    def emit_v_sub(tc, vs, half):
        xr = dep["xnT_ready"][tc]
        c_mm = None
        for kk in range(4 * half, 4 * half + 4):
            w = (("dv", xr), ("po", dep["xnT_po"][tc]), ("ac", dep["xnT_ac"][tc]),
                 ("wcw", dep["w_qkv"]), dep["pm_free"]) if kk == 0 else ()
            c_mm = sc.emit("tensor", lambda t, kk=kk, vs=vs: t.matmul(
                pm[:], xnq[:, kk, 128 * vs:128 * vs + 128], Wv[:, kk, :],
                start=(kk == 0), stop=(kk == 7)), waits=w, inc="pe" if kk == 7 else None)
        if half == 0:
            return
        dep["xnq_rd"] = c_mm
        jb = 4 * tc + vs
        c_ev = sc.emit("vector", lambda v, jb=jb: v.tensor_copy(
            v_aug[:, jb, :].rearrange("p (h e) -> p h e", h=8)[:, :, 0:64],
            pm[:].rearrange("p (h e) -> p h e", h=8)),
            waits=(("pe", c_mm), ("dv", dep["v_ones"])), inc="dv")
        dep["pm_free"] = ("dv", c_ev)

    def emit_ffn_pre(tc):
        c_mm = sc.emit("tensor", lambda t: t.matmul(
            pm[:], rows[0:1, 512:640], rows[0:1, 0:512], start=True, stop=True),
            waits=(("dv", max(dep["negxsc_row"], dep["ones_r"])), dep["pm_free"],
                   ("ldw", dep["w_small"])), inc="pe")
        dep["rows0_rd"] = c_mm
        c_cp = sc.emit("scalar", lambda s: s.activation(xscBs[:], pm[:], AF.Copy),
                       waits=(("pe", c_mm),), inc="ac")
        dep["pm_free"] = ("ac", c_cp)
        dep["xscB_ready"] = ("ac", c_cp)

    def emit_ffn_oc(tc, oc, half):
        xr = dep["xqT_ready"][tc]
        c_mm = None
        for kk in range(4 * half, 4 * half + 4):
            w = (("dv", xr), ("ac", dep["xnT_ac"][tc]), ("dv", dep["w_qT"]),
                 dep["pm_free"], dep["xscB_ready"]) if kk == 0 else ()
            c_mm = sc.emit("tensor", lambda t, kk=kk, oc=oc: t.matmul(
                pm[:], wqT[:, kk, 128 * oc:128 * oc + 128], xnq[:, 8 + kk, :],
                start=(kk == 0), stop=(kk == 7)), waits=w, inc="pe" if kk == 7 else None)
        if half == 0:
            return
        c_g1 = sc.emit("vector", lambda v: v.tensor_tensor(
            g1s[:], pm[:], xscBs[:], ALU.mult),
            waits=(("pe", c_mm), ("ac", dep["g1_free"])), inc="dv")
        dep["pm_free"] = ("dv", c_g1)
        c_z = sc.emit("gpsimd", lambda g, oc=oc: g.tensor_scalar(
            t_b[:], g1s[:], wscb[:, oc:oc + 1], biab[:, oc:oc + 1],
            ALU.mult, ALU.add),
            waits=(("dv", max(c_g1, dep["t_b_rd"])), ("ac", dep["g1_free"])),
            inc="po")
        dep["z_po"] = c_z
        c_th = sc.emit("scalar", lambda s: s.activation(
            g1s[:], t_b[:], AF.Tanh, scale=0.5),
            waits=(("po", c_z), ("ldw", dep["w_small"])), inc="ac")
        dep["g1_free"] = c_th
        gsl = TC * (tc % 4)
        c_si = sc.emit("vector", lambda v, oc=oc, gsl=gsl: v.scalar_tensor_tensor(
            gT[:, oc, gsl:gsl + TC], g1s[:], 1.0, t_b[:],
            ALU.add, ALU.mult),
            waits=(("ac", c_th), ("dv", dep["gT_rd"][tc % 4])), inc="dv")
        dep["t_b_rd"] = c_si
        return c_si

    def qq(i):
        dep["a2q"][i] = emit_tile_a2_quant(i)

    def qq2(i):
        dep["a2q"][i] = emit_tile_a2_quant(i, (1,))

    def a_rest_steps(tc):
        steps = []
        for j in range(4):
            if tc < NMERGE:
                steps.append((lambda i=4 * tc + j: emit_xn_copy(i), None,
                              4 * tc + j + 1))
                steps.append((lambda i=4 * tc + j: emit_tile_a2_pe(i), None,
                              4 * tc + j + 2))
            else:
                steps.append((lambda i=4 * tc + j: emit_tile_a2_pe_h(i, 0), None, 0))
                steps.append((lambda i=4 * tc + j: emit_tile_a2_pe_h(i, 1), None, 0))

        def fin_tiles(tc=tc):
            dep["xnT_ready"][tc] = sc.cnt["dv"]
            dep["xqT_ready"][tc] = sc.cnt["dv"]
            dep["xnT_po"][tc] = sc.cnt["po"]
            dep["xnT_ac"][tc] = sc.cnt["ac"]
            emit_negxsc_row(tc)
        steps.append((fin_tiles, f"tilesdone{tc}", 4 * tc + 5))
        for vs in range(4):
            g = 4 * tc + 5
            steps.append((lambda vs=vs, tc=tc: emit_v_sub(tc, vs, 0), "open", g))
            steps.append((lambda vs=vs, tc=tc: emit_v_sub(tc, vs, 1), None, g))

        def fin_v(tc=tc):
            dep["v"][tc] = sc.cnt["dv"]
        steps.append((fin_v, f"vdone{tc}", 0))
        steps.append((lambda tc=tc: emit_ffn_pre(tc), None, 0))
        for oc in range(4):
            g = 4 * tc + 5
            steps.append((lambda oc=oc, tc=tc: emit_ffn_oc(tc, oc, 0), "open", g))
            steps.append((lambda oc=oc, tc=tc: emit_ffn_oc(tc, oc, 1), None, g))

        def fin_g(tc=tc):
            dep["g"][tc] = sc.cnt["dv"]
        steps.append((fin_g, None, 0))
        return steps

    def a_q_steps(tc):
        steps = []
        for f in range(4):
            steps.append((lambda f=f, tc=tc: emit_qkv_f(tc, f, 0), "open",
                          4 * tc + 5))
            steps.append((lambda f=f, tc=tc: emit_qkv_f(tc, f, 1), None,
                          4 * tc + 5))

        def fin_qk(tc=tc):
            dep["qkT"][tc] = sc.cnt["ac"]
        steps.append((fin_qk, f"qkdone{tc}", 0))
        return steps

    def a_substeps(tc):
        steps = []
        tiles = [4 * tc + j for j in range(4)]

        def dma_pair(i):
            emit_x_dma(2 * NT + 2 * i)
            emit_x_dma(2 * NT + 2 * i + 1)

        def sd(j):
            i = tiles[j]
            dma_pair(i)
            emit_tile_a2_stats(i)
        rest = [(fn, tag) if tag else fn for fn, tag, _g in a_rest_steps(tc)]
        pes, rest = rest[:8], rest[8:]
        order = [("s", 0), ("qa", 0), ("qb", 0), ("s", 1), ("qa", 1),
                 ("p", 0), ("qb", 1), ("p", 1), ("s", 2), ("qa", 2),
                 ("p", 2), ("qb", 2), ("s", 3), ("p", 3), ("qa", 3),
                 ("p", 4), ("qb", 3), ("p", 5), ("p", 6), ("p", 7)]
        for kind, j in order:
            if kind == "s":
                steps.append(lambda j=j: sd(j))
            elif kind == "qa":
                steps.append(lambda i=tiles[j]: emit_tile_a2_quant(i, (0,), (0,)))
                steps.append(lambda i=tiles[j]: emit_tile_a2_quant(i, (0,), (1,)))
            elif kind == "qb":
                steps.append(lambda i=tiles[j]: emit_tile_a2_quant(i, (1,), (0,)))
                steps.append(lambda i=tiles[j]: emit_tile_a2_quant(i, (1,), (1,)))
            else:
                steps.append(pes[j])
        for st in rest:
            steps.append(st)
        stepsB = [(fn, tag) if tag else fn for fn, tag, _g in a_q_steps(tc)]
        return steps, stepsB

    def emit_attention_chunk(c, pump, close_open, drain_pre, drain_full):
        for P in range(NPAIR):
            if P == 1:
                drain_full()
            njb = 4 * c + 4
            c_sc_last = None
            pend_av = []
            last_av = [None]

            def flush_av():
                if pend_av:
                    pend_av.pop(0)()

            pops_at = (0, 1, 2, 3) if njb <= 4 else (0, 2, 3, 4)
            for jb in range(njb):
                if P == 0 and jb == 4 * c - 1:
                    drain_pre()
                if att["pend_gate"] and jb in pops_at:
                    close_open()
                    att["pend_gate"].pop(0)()
                if jb >= 3:
                    flush_av()
                par = att["unit"] % 2
                pe3 = att["unit"] % 3
                o = max(0, 128 * jb - TC * c)
                w = TC - o
                is_diag = jb >= 4 * c
                qb = (c % 3) * TC
                w0 = (("ac", max(att["sT_rd"][par], dep["qkT"][c], dep["k_all_ac"])),
                      ("dv", dep["k_all"]))
                if not is_diag:
                    sc.emit("tensor", lambda t, P=P, jb=jb, o=o, w=w, par=par, qb=qb:
                            t.matmul(
                        sT[par][:, o:o + w],
                        kTa[0:64, P, 128 * jb:128 * jb + 128],
                        qT[P][0:64, qb + o:qb + o + w],
                        start=True, stop=True), waits=w0)
                    c_sc = sc.emit("tensor", lambda t, P=P, jb=jb, o=o, w=w, par=par,
                                   qb=qb: t.matmul(
                        sT[par][:, 512 + o:512 + o + w],
                        kTa[64:128, P, 128 * jb:128 * jb + 128],
                        qT[P][64:128, qb + o:qb + o + w],
                        start=True, stop=True), inc="pe")
                else:
                    sc.emit("tensor", lambda t, P=P, jb=jb, o=o, par=par, qb=qb:
                            t.matmul(
                        sT[par][:, o:o + 128],
                        kTa[0:64, P, 128 * jb:128 * jb + 128],
                        qT[P][0:64, qb + o:qb + o + 128],
                        start=True, stop=False), waits=w0)
                    sc.emit("tensor", lambda t, P=P, jb=jb, o=o, par=par, qb=qb:
                            t.matmul(
                        sT[par][:, 512 + o:512 + o + 128],
                        kTa[64:128, P, 128 * jb:128 * jb + 128],
                        qT[P][64:128, qb + o:qb + o + 128],
                        start=True, stop=False))
                    sc.emit("tensor", lambda t, o=o, par=par: t.matmul(
                        sT[par][:, o:o + 128], idb[:], nmk[:],
                        start=False, stop=True))
                    c_sc = sc.emit("tensor", lambda t, o=o, par=par: t.matmul(
                        sT[par][:, 512 + o:512 + o + 128], idb[:], nmk[:],
                        start=False, stop=True), inc=None if w > 128 else "pe")
                    if w > 128:
                        sc.emit("tensor", lambda t, P=P, jb=jb, o=o, w=w, par=par,
                                qb=qb: t.matmul(
                            sT[par][:, o + 128:o + w],
                            kTa[0:64, P, 128 * jb:128 * jb + 128],
                            qT[P][0:64, qb + o + 128:qb + o + w],
                            start=True, stop=True))
                        c_sc = sc.emit("tensor", lambda t, P=P, jb=jb, o=o, w=w,
                                       par=par, qb=qb: t.matmul(
                            sT[par][:, 512 + o + 128:512 + o + w],
                            kTa[64:128, P, 128 * jb:128 * jb + 128],
                            qT[P][64:128, qb + o + 128:qb + o + w],
                            start=True, stop=True), inc="pe")
                c_sc_last = c_sc
                c_ex = sc.emit("scalar", lambda s, par=par, pe3=pe3, o=o, w=w:
                               s.activation(
                    eT[pe3][:].rearrange("p (b n) -> p b n", b=2)[:, :, o:o + w],
                    sT[par][:].rearrange("p (b n) -> p b n", b=2)[:, :, o:o + w],
                    AF.Exp),
                    waits=(("pe", c_sc), ("pe", att["eT_rd"][pe3])), inc="ac")
                att["sT_rd"][par] = c_ex
                avw = [("ac", c_ex), ("dv", dep["v"][jb // 4])]
                hA, hB = 2 * P, 2 * P + 1

                def mk_av(jb=jb, o=o, w=w, pe3=pe3, hA=hA, hB=hB, njb=njb, avw=tuple(avw)):
                    def go():
                        ws = list(avw)
                        if jb == 0:
                            ws.append(("dv", att["O_rd"]))
                        sc.emit("tensor", lambda t: t.matmul(
                            Oa[:, o:o + w], v_aug[:, jb, 65 * hA:65 * hA + 65],
                            eT[pe3][:, o:o + w],
                            start=(jb == 0), stop=(jb == njb - 1)), waits=ws)
                        c_av = sc.emit("tensor", lambda t: t.matmul(
                            Ob[:, o:o + w], v_aug[:, jb, 65 * hB:65 * hB + 65],
                            eT[pe3][:, 512 + o:512 + o + w],
                            start=(jb == 0), stop=(jb == njb - 1)), inc="pe")
                        att["eT_rd"][pe3] = c_av
                        last_av[0] = c_av
                    return go
                pend_av.append(mk_av())
                att["unit"] += 1
                pump()
            while pend_av:
                pend_av.pop(0)()
            dep["att_q_free"][P][c % 3] = c_sc_last
            gst = {}

            def ga1(c=c, P=P, av=last_av[0], gst=gst):
                gst["lw"] = emit_gating_a1(c, P, av)

            def ga2(c=c, P=P, gst=gst):
                gst["le"] = emit_gating_a2(c, P, gst["lw"][0])

            def gb1(c=c, P=P, gst=gst):
                gst["xr"] = emit_gating_b1(c, P)

            def gb2(c=c, P=P, gst=gst):
                emit_gating_b2(c, P, gst["le"], gst["xr"], gst["lw"][1])
            att["pend_gate"] += [ga1, ga2, gb1, gb2]

    def emit_gating_a1(c, P, c_av_last):
        sc.emit("scalar", lambda s: s.activation(
            rows[64:65, 0:512], Oa[64:65, :], AF.Copy),
            waits=(("pe", c_av_last), ("pe", dep["rows64_rd"])))
        c_l = sc.emit("scalar", lambda s: s.activation(
            rows[64:65, 512:1024], Ob[64:65, :], AF.Copy), inc="ac")
        gsl = TC * (c % 4)
        sc.emit("vector", lambda v, P=P, gsl=gsl: v.tensor_tensor(
            wpr[0:64, :], Oa[0:64, :], gT[0:64, P, gsl:gsl + TC], ALU.mult),
            waits=(("pe", c_av_last), ("dv", dep["g"][c]),
                   ("pe", dep["wpr_free"])))
        c_w = sc.emit("vector", lambda v, P=P, gsl=gsl: v.tensor_tensor(
            wpr[64:128, :], Ob[0:64, :], gT[64:128, P, gsl:gsl + TC], ALU.mult),
            inc="dv")
        att["O_rd"] = c_w
        if P == NPAIR - 1:
            dep["gT_rd"][c % 4] = c_w
        return c_l, c_w

    def emit_gating_a2(c, P, c_l):
        c_lt = None
        for it in range(4):
            c_lt = sc.emit("tensor", lambda t, it=it: t.matmul(
                pm[0:128, 2 * it:2 * it + 1], rows[64:65, 128 * it:128 * it + 128],
                idf[64:65, 64:65], start=True, stop=True),
                waits=(("ac", c_l), dep["pm_free"]) if it == 0 else ())
            c_lt = sc.emit("tensor", lambda t, it=it: t.matmul(
                pm[0:128, 2 * it + 1:2 * it + 2],
                rows[64:65, 512 + 128 * it:512 + 128 * it + 128],
                idf[64:65, 64:65], start=True, stop=True),
                inc="pe" if it == 3 else None)
        dep["rows64_rd"] = c_lt
        c_le = sc.emit("vector", lambda v: v.reciprocal(linvT[:, 0:8], pm[0:128, 0:8]),
                       waits=(("pe", c_lt),), inc="dv")
        dep["pm_free"] = ("dv", c_le)
        return c_le

    def emit_gating_b1(c, P):
        par = (c * NPAIR + P) % 2
        c_xr = sc.emit("gpsimd", lambda g, c=c, P=P, par=par: g.dma_start(
            out=xres[par][:].rearrange("p (it f) -> p it f", it=4),
            in_=xr_h[TC * c:TC * c + TC, :].rearrange("(it p) d -> p it d", p=128)
                 [:, :, 128 * P:128 * P + 128]),
            waits=(("dv", dep["xres_free"][par]),), inc=f"xr{par}", amt=16)
        return c_xr

    def emit_gating_b2(c, P, c_le, c_xr, c_w):
        par = (c * NPAIR + P) % 2
        c_wt = None
        for it in range(4):
            c_wt = sc.emit("tensor", lambda t, it=it: t.transpose(
                pt[:, 128 * it:128 * it + 128], wpr[:, 128 * it:128 * it + 128], idb[:]),
                waits=(("dv", c_w), dep["pt_free"]) if it == 0 else (),
                inc="pe" if it == 3 else None)
        dep["pt_user"] = c_wt
        dep["wpr_free"] = c_wt
        c_f = None
        for it in range(4):
            c_f = sc.emit("vector", lambda v, it=it, par=par: v.scalar_tensor_tensor(
                outs[par][:, 128 * it:128 * it + 64],
                pt[:, 128 * it:128 * it + 64], linvT[:, 2 * it:2 * it + 1],
                xres[par][:, 128 * it:128 * it + 64], ALU.mult, ALU.add),
                waits=(("pe", c_wt), (f"xr{par}", c_xr), ("dv", c_le),
                       (f"out{par}", dep["outs_free"][par])) if it == 0 else ())
            c_f = sc.emit("vector", lambda v, it=it, par=par: v.scalar_tensor_tensor(
                outs[par][:, 128 * it + 64:128 * it + 128],
                pt[:, 128 * it + 64:128 * it + 128], linvT[:, 2 * it + 1:2 * it + 2],
                xres[par][:, 128 * it + 64:128 * it + 128], ALU.mult, ALU.add),
                inc="dv" if it == 3 else None)
        dep["pt_free"] = ("dv", c_f)
        dep["xres_free"][par] = c_f
        c_out = sc.emit("gpsimd", lambda g, c=c, P=P, par=par: g.dma_start(
            out=out_h[TC * c:TC * c + TC, :].rearrange("(it p) d -> p it d", p=128)
                 [:, :, 128 * P:128 * P + 128],
            in_=outs[par][:].rearrange("p (it f) -> p it f", it=4)),
            waits=(("dv", c_f),), inc=f"out{par}", amt=16)
        dep["outs_free"][par] = c_out

    emit_weights()
    M = []
    for tc in range(NMERGE):
        M += a_rest_steps(tc)
        if tc < 3:
            M += a_q_steps(tc)
    mst = {"k": 0}

    def pump_merged(i, drain=False):
        target = len(M) if drain else min(
            len(M), (max(0, i - 2) * len(M)) // max(1, NT + 6))
        while mst["k"] < len(M):
            fn, tag, gate = M[mst["k"]]
            tag = tag if tag == "open" else None
            if not drain and gate > i - 1 and not (mst["k"] < target and gate <= i):
                break
            fn()
            mst["k"] += 1
            while mst["k"] < len(M) and M[mst["k"] - 1][1] == "open":
                M[mst["k"]][0]()
                mst["k"] += 1

    emit_x_dma(0)
    emit_x_dma(1)
    emit_x_dma(2)
    emit_a1_stats(0)
    for i in range(NT):
        pump_merged(i)
        emit_tile_a1(i)
        emit_a2_affine(i, True)
        for h in (2 * i + 3, 2 * i + 4):
            if h < 2 * NT:
                emit_x_dma(h)
        if i + 1 < NT:
            emit_a1_stats(i + 1)
        if 1 <= i and i - 1 < 4 * NMERGE:
            qq(i - 1)
        if i < 4 * NMERGE:
            emit_tile_a2_stats(i, merged=True)
        else:
            emit_prestats(i)
    for k in range(NT - 1, 4 * NMERGE):
        qq(k)
    if dep["a1_scatter"] is not None:
        dep["a1_scatter"]()
        dep["a1_scatter"] = None
    dep["k_all"] = sc.cnt["dv"]
    dep["k_all_ac"] = 0

    def norm(raw):
        return [e if isinstance(e, tuple) else (e, None) for e in raw]

    QA = []
    qa_end = {}
    flatA = []
    qb = {}

    qa_tiles_end = {}
    qa_pre_end = {}

    def append_tc(tc):
        sa, sb = a_substeps(tc)
        sa = norm(sa)
        for j, (fn, tag) in enumerate(sa):
            if tag and tag.startswith("tilesdone"):
                qa_tiles_end[tc] = len(flatA) + j + 1
                sa[j] = (fn, None)
            elif tag and tag.startswith("vdone"):
                qa_pre_end[tc] = len(flatA) + j + 1
                sa[j] = (fn, None)
        flatA.extend(sa)
        qa_end[tc] = len(flatA)
        qb[tc] = norm(sb)

    for tcq in range(3, NMERGE):
        if tcq < NTC:
            qb[tcq] = norm([(fn, tag) if tag else fn
                            for fn, tag, _g in a_q_steps(tcq)])
    qa_start = {}
    if mst["k"] < len(M):
        if NTC <= 2:
            pump_merged(NT, drain=True)
        else:
            for fn, tag, _g in M[mst["k"]:]:
                keep = tag if tag == "open" else None
                flatA.append((fn, keep))
                if tag and tag.startswith("qkdone"):
                    t = int(tag[6:])
                    qa_start[t] = qa_pre_end[t] = qa_end[t] = len(flatA)
                elif tag and tag.startswith("vdone"):
                    qa_pre_end[int(tag[5:])] = len(flatA)
                elif tag and tag.startswith("tilesdone"):
                    qa_tiles_end[int(tag[9:])] = len(flatA)
            mst["k"] = len(M)
            qa_end[min(NMERGE, NTC) - 1] = len(flatA)
    ia = {"k": 0}
    ib = {"k": 0, "lst": []}

    def stepA():
        fn, tag = flatA[ia["k"]]
        fn()
        ia["k"] += 1
        return tag

    def stepB():
        fn, tag = ib["lst"][ib["k"]]
        fn()
        ib["k"] += 1
        return tag

    def drain_open():
        while (ia["k"] > 0 and ia["k"] < len(flatA)
               and flatA[ia["k"] - 1][1] == "open"):
            stepA()
        while (ib["k"] > 0 and ib["k"] < len(ib["lst"])
               and ib["lst"][ib["k"] - 1][1] == "open"):
            stepB()

    for c in range(NTC):
        def drain_pre(c=c):
            while ia["k"] < qa_pre_end.get(c, qa_end.get(c, 0)):
                stepA()

        def drain_full(c=c):
            while ia["k"] < qa_end.get(c, 0):
                stepA()
        while ia["k"] < qa_start.get(c, 0):
            stepA()
        if c == 0:
            drain_pre()
        while ib["k"] < len(ib["lst"]):
            stepB()
        if NMERGE <= c + 2 < NTC:
            append_tc(c + 2)
        ib["lst"] = qb.get(c + 2, [])
        ib["k"] = 0
        ib["gate"] = qa_tiles_end.get(c + 2, 0)

        uc = {"u": 0}
        nunits = NPAIR * (4 * c + 4)
        dlA = 0

        def pump():
            uc["u"] += 1
            left = max(1, nunits - uc["u"] + 1)
            needed = max(0, dlA - ia["k"]) + (len(ib["lst"]) - ib["k"] + 1) // 2
            budget = max(1, -(-needed // left))
            for _ in range(budget):
                if ib["k"] < len(ib["lst"]) and ia["k"] >= ib["gate"]:
                    stepB()
                elif ia["k"] < len(flatA):
                    stepA()
                else:
                    break
            drain_open()

        def close_open():
            drain_open()
        emit_attention_chunk(c, pump, close_open, drain_pre, drain_full)
    while ia["k"] < len(flatA):
        stepA()
    while ib["k"] < len(ib["lst"]):
        stepB()

    while att["pend_gate"]:
        att["pend_gate"].pop(0)()
    sc.emit("gpsimd", lambda g: g.engine_nop(),
            waits=(("out0", sc.cnt["out0"]), ("out1", sc.cnt["out1"])))

    with nc.Block() as block:
        @block.sync
        def _(s):
            sc.replay("sync", s, sems)

        @block.gpsimd
        def _(g):
            sc.replay("gpsimd", g, sems)

        @block.vector
        def _(v):
            sc.replay("vector", v, sems)

        @block.scalar
        def _(s):
            sc.replay("scalar", s, sems)

        @block.tensor
        def _(t):
            sc.replay("tensor", t, sems)

    stack.close()
    return nc


_NC_CACHE = {}


def _get_nc(S):
    if S not in _NC_CACHE:
        _NC_CACHE[S] = build_program(S)
    return _NC_CACHE[S]


def kernel(L0_attn_x, L0_attn_qkv_weights, L0_ffn_weight_q, L0_ffn_weight_scale,
           L0_ffn_bias):
    x = np.asarray(L0_attn_x)
    W = np.asarray(L0_attn_qkv_weights)
    w8 = np.asarray(L0_ffn_weight_q)
    wsc = np.asarray(L0_ffn_weight_scale)
    bia = np.asarray(L0_ffn_bias)
    B, S, D_ = x.shape
    assert D_ == D

    ident_b = np.eye(128, dtype=ml_dtypes.bfloat16)
    ident_f = np.eye(128, dtype=np.float32)
    j = np.arange(128)
    maskT = (j[:, None] <= j[None, :]).astype(ml_dtypes.bfloat16)
    onesf = np.ones((128, 128), dtype=np.float32)
    pswap = np.zeros((128, 128), dtype=ml_dtypes.bfloat16)
    pswap[(np.arange(128) + 64) % 128, np.arange(128)] = 1
    negmask = np.where(j[:, None] > j[None, :], -1e9, 0.0).astype(ml_dtypes.bfloat16)

    nc = _get_nc(S)
    in_maps = []
    for c in range(8):
        b, half = c // 2, c % 2
        hs = DL * half
        in_maps.append({
            "x": np.ascontiguousarray(x[b]),
            "xr": np.ascontiguousarray(x[b][:, hs:hs + DL]),
            "wq": np.ascontiguousarray(W[:, hs:hs + DL]),
            "wk": np.ascontiguousarray(W[:, D + hs:D + hs + DL]),
            "wv": np.ascontiguousarray(W[:, 2 * D + hs:2 * D + hs + DL]),
            "w8": np.ascontiguousarray(w8[hs:hs + DL, :]),
            "wsc": np.ascontiguousarray(wsc[hs:hs + DL]),
            "bia": np.ascontiguousarray(bia[hs:hs + DL]),
            "idb": ident_b, "idf": ident_f,
            "psw": pswap, "nmk": negmask,
        })
    res = run_bass_kernel_spmd(nc, in_maps, core_ids=list(range(8)))
    out = np.empty((B, S, D), dtype=np.float32)
    for c in range(8):
        b, half = c // 2, c % 2
        out[b][:, DL * half:DL * half + DL] = np.asarray(res.results[c]["out"])
    return out
